# version 01 (final)
# baseline (speedup 1.0000x reference)
import contextlib

import numpy as np
import concourse.bass as bass
import concourse.mybir as mybir
from concourse.bass_utils import run_bass_kernel_spmd

F32 = mybir.dt.float32
BF16 = mybir.dt.bfloat16
AF = mybir.ActivationFunctionType
AX = mybir.AxisListType
ALU = mybir.AluOpType

NCORES = 8
D = 1024
SEQ = 2048
TCORE = 4096
NCH = 8
GRAN = 2048

ENGINES = ("pe", "act", "dve", "pool", "sp")


class _Op:
    __slots__ = ("eng", "fn", "idx", "deps", "inc", "is_dma", "sem_id",
                 "dma_cnt", "tag")

    def __init__(self, eng, fn, idx, tag):
        self.eng = eng
        self.fn = fn
        self.idx = idx
        self.deps = []
        self.inc = False
        self.is_dma = False
        self.sem_id = None
        self.dma_cnt = None
        self.tag = tag


class Prog:
    def __init__(self, nc, n_dma_sems=20):
        self.nc = nc
        self.st = contextlib.ExitStack()
        self.ops = {e: [] for e in ENGINES}
        self.last_writer = {}
        self.readers = {}
        self.vc = {e: {} for e in ENGINES}
        self.dma_queues = {"sp": [], "pool": [], "act": []}
        self.dma_cnts = {}
        self.dma_rr = {"sp": 0, "pool": 0, "act": 0}
        self.n_dma_sems = n_dma_sems
        self.eng_sems = {e: self.st.enter_context(nc.semaphore(f"S_{e}"))
                         for e in ENGINES}
        self._semid = 0
        self.dma_sem_handles = {}
        self.alloc_names = set()
        self.hot_keys = set()
        self.hot_names = set()
        self.fence_all = False

    def sbuf(self, name, shape, dtype):
        assert name not in self.alloc_names, name
        self.alloc_names.add(name)
        return self.st.enter_context(self.nc.sbuf_tensor(name, list(shape), dtype))

    def psum(self, name, shape, dtype=F32):
        assert name not in self.alloc_names, name
        self.alloc_names.add(name)
        return self.st.enter_context(self.nc.psum_tensor(name, list(shape), dtype))

    def _new_dma_sem(self, q):
        h = self.st.enter_context(self.nc.semaphore(f"D_{q}{self._semid}"))
        sid = self._semid
        self._semid += 1
        self.dma_sem_handles[sid] = h
        self.dma_cnts[sid] = 0
        return sid

    def _key_conflicts(self, k, store):
        name, idx = k
        if idx is None:
            for k2, v in store.items():
                if k2[0] == name:
                    yield v
        else:
            v = store.get(k)
            if v is not None:
                yield v
            v = store.get((name, None))
            if v is not None:
                yield v

    def _reader_conflicts(self, k):
        name, idx = k
        out = []
        if idx is None:
            for k2, v in self.readers.items():
                if k2[0] == name:
                    out.extend(v)
        else:
            out.extend(self.readers.get(k, ()))
            out.extend(self.readers.get((name, None), ()))
        return out

    def _record(self, eng, fn, reads, writes, tag, sreads=()):
        op = _Op(eng, fn, len(self.ops[eng]), tag)
        dep_ops = []
        for k in reads:
            dep_ops.extend(self._key_conflicts(k, self.last_writer))
        for k in writes:
            dep_ops.extend(self._key_conflicts(k, self.last_writer))
            dep_ops.extend(self._reader_conflicts(k))
        fence = self.fence_all or any(
            k in self.hot_keys or k[0] in self.hot_names
            for k in list(reads) + list(writes))
        sdep_ops = []
        for k in sreads:
            sdep_ops.extend(self._key_conflicts(k, self.last_writer))
        if fence:
            for k in reads:
                sdep_ops.extend(self._key_conflicts(k, self.last_writer))
            for k in writes:
                sdep_ops.extend(self._key_conflicts(k, self.last_writer))
                sdep_ops.extend(self._reader_conflicts(k))
        max_dma = {}
        max_eng = {}
        max_self = None
        for d in sdep_ops:
            if d is op or d.is_dma:
                dep_ops.append(d)
                continue
            if d.eng == eng:
                if max_self is None or d.idx > max_self[0]:
                    max_self = (d.idx, d)
            else:
                dep_ops.append(d)
        for d in dep_ops:
            if d is op:
                continue
            if d.is_dma:
                cur = max_dma.get(d.sem_id)
                if cur is None or d.dma_cnt > cur[0]:
                    max_dma[d.sem_id] = (d.dma_cnt, d)
            else:
                if d.eng == eng:
                    continue
                cur = max_eng.get(d.eng)
                if cur is None or d.idx > cur[0]:
                    max_eng[d.eng] = (d.idx, d)
        vc = self.vc[eng]
        for sid, (cnt, d) in max_dma.items():
            cid = ("d", sid)
            if vc.get(cid, 0) >= cnt:
                continue
            vc[cid] = cnt
            op.deps.append(("dma", sid, cnt))
        for e2, (idx, d) in max_eng.items():
            cid = ("e", e2)
            if vc.get(cid, 0) >= idx + 1:
                continue
            vc[cid] = idx + 1
            d.inc = True
            op.deps.append(("eng", e2, idx))
        if max_self is not None:
            cid = ("self", eng)
            if vc.get(cid, 0) < max_self[0] + 1:
                vc[cid] = max_self[0] + 1
                max_self[1].inc = True
                op.deps.append(("eng", eng, max_self[0]))
        for k in reads:
            self.readers.setdefault(k, []).append(op)
        for k in writes:
            self.last_writer[k] = op
            self.readers[k] = []
        self.ops[eng].append(op)
        return op

    @staticmethod
    def _norm(k):
        return k if isinstance(k, tuple) else (k, None)

    def op(self, eng, fn, reads=(), writes=(), tag="", sreads=()):
        return self._record(eng, fn, [self._norm(k) for k in reads],
                            [self._norm(k) for k in writes], tag,
                            sreads=[self._norm(k) for k in sreads])

    def dma(self, queue, out_ap, in_ap, reads=(), writes=(), tag=""):
        q = queue
        pool = self.dma_queues[q]
        if len(pool) < self.n_dma_sems:
            sid = self._new_dma_sem(q)
            pool.append(sid)
        else:
            sid = pool[self.dma_rr[q] % len(pool)]
        self.dma_rr[q] += 1
        prev_cnt = self.dma_cnts[sid]

        def fn(e, _out=out_ap, _in=in_ap, _sid=sid, _prev=prev_cnt):
            if _prev > 0:
                e.wait_ge(self.dma_sem_handles[_sid], 16 * _prev)
            return e.dma_start(out=_out, in_=_in)

        op = self._record(q, fn, [self._norm(k) for k in reads],
                          [self._norm(k) for k in writes], tag)
        op.is_dma = True
        op.sem_id = sid
        self.dma_cnts[sid] = prev_cnt + 1
        op.dma_cnt = prev_cnt + 1
        return op

    def emit(self):
        nc = self.nc
        inc_count = {}
        for e in ENGINES:
            c = 0
            for o in self.ops[e]:
                if o.inc and not o.is_dma:
                    c += 1
                inc_count[(e, o.idx)] = c
        blk = self.st.enter_context(nc.Block())

        def emit_engine(eng_obj, e):
            for o in self.ops[e]:
                for d in o.deps:
                    if d[0] == "dma":
                        eng_obj.wait_ge(self.dma_sem_handles[d[1]], 16 * d[2])
                    else:
                        eng_obj.wait_ge(self.eng_sems[d[1]],
                                        inc_count[(d[1], d[2])])
                ins = o.fn(eng_obj)
                if o.is_dma:
                    ins.then_inc(self.dma_sem_handles[o.sem_id], 16)
                elif o.inc:
                    ins.then_inc(self.eng_sems[e], 1)
            if e == "sp":
                for sid, cnt in self.dma_cnts.items():
                    if cnt > 0:
                        eng_obj.wait_ge(self.dma_sem_handles[sid], 16 * cnt)
                for e2 in ENGINES:
                    if e2 == "sp" or not self.ops[e2]:
                        continue
                    tot = inc_count[(e2, len(self.ops[e2]) - 1)]
                    if tot > 0:
                        eng_obj.wait_ge(self.eng_sems[e2], tot)

        blk.tensor(lambda te: emit_engine(te, "pe"))
        blk.scalar(lambda sc: emit_engine(sc, "act"))
        blk.vector(lambda ve: emit_engine(ve, "dve"))
        blk.gpsimd(lambda gp: emit_engine(gp, "pool"))
        blk.sync(lambda sy: emit_engine(sy, "sp"))
        self.st.close()

    def stats(self):
        n = {e: len(self.ops[e]) for e in ENGINES}
        n["waits"] = sum(len(o.deps) for e in ENGINES for o in self.ops[e])
        n["incs"] = sum(1 for e in ENGINES for o in self.ops[e] if o.inc)
        return n


class Buf:

    def __init__(self, ctx, nch, T, dtype=BF16, hot=False):
        self.ctx = ctx
        self.nch = nch
        self.T = T
        self.dt = dtype
        self.esz = 2 if dtype == BF16 else 4
        nbytes = nch * T * self.esz
        self.g0 = ctx.ar_alloc(nbytes)
        if hot:
            ng = (nbytes + GRAN - 1) // GRAN
            for g in range(self.g0, self.g0 + ng):
                ctx.p.hot_keys.add(("AR", g))

    def ap(self, c=0, t0=0, tl=None):
        tl = self.T - t0 if tl is None else tl
        b0 = self.g0 * GRAN + (c * self.T + t0) * self.esz
        col0 = b0 // 2
        ap = self.ctx.art[:, col0: col0 + tl * self.esz // 2]
        if self.dt != BF16:
            ap = ap.bitcast(F32)
        return ap

    def keys(self, c=None, t0=0, tl=None):
        if c is None:
            lo = self.g0
            hi = self.g0 + (self.nch * self.T * self.esz + GRAN - 1) // GRAN
        else:
            tl = self.T - t0 if tl is None else tl
            b0 = self.g0 * GRAN + (c * self.T + t0) * self.esz
            b1 = b0 + tl * self.esz
            lo, hi = b0 // GRAN, (b1 + GRAN - 1) // GRAN
        return [("AR", g) for g in range(lo, hi)]


class Ctx:
    def __init__(self, nc, p, host):
        self.nc = nc
        self.p = p
        self.host = host
        self.inputs_used = set()
        self._dram = {}
        self.uid = 0
        self.psb = [p.psum(f"psb{i}", [128, 512], F32) for i in range(8)]
        self.pskey = [f"psb{i}" for i in range(8)]
        self.ones_f = self.const_sb("c_ones", np.ones((128, 128), np.float32), F32)
        self.ones_b = self.const_sb("c_onesb", np.ones((128, 128), np.float32), BF16)
        self.ident_f = self.const_sb("c_ident", np.eye(128, dtype=np.float32), F32)
        hsel = np.zeros((2, 128), np.float32)
        hsel[0, 0:64] = 1.0
        hsel[1, 64:128] = 1.0
        self.hsel = self.const_sb("c_hsel", hsel, F32)
        tri = np.tril(np.ones((128, 128), np.float32))
        self.tri01 = self.const_sb("c_tri01", tri.T.copy(), BF16)
        self.nident_f = self.const_sb("c_nident", -np.eye(128, dtype=np.float32), F32)
        self.wb = [p.sbuf(f"wb{i}", [128, 512], BF16) for i in range(12)]
        self.wb_rr = 0
        self.rows = p.sbuf("rows", [1, 1536], F32)
        self.vstg_sh = p.sbuf("vstg_sh", [32, 128], F32)
        self.vstg2_sh = p.sbuf("vstg2_sh", [1, 128], F32)
        self.zrow = p.sbuf("zrow2", [2, 512], F32)
        self.fc = p.sbuf("fconsts", [128, 16], F32)
        self.fc_map = {}
        rem = nc.sbuf_bytes_remaining
        arena_bpp = (rem - 10240) // GRAN * GRAN
        self.ar_ng = arena_bpp // GRAN
        self.art = p.sbuf("AR", [128, arena_bpp // 2], BF16)
        self.ar_cur = 0
        self.ar_marks = []
        self.ar_high = 0

    def ar_alloc(self, nbytes_per_part):
        g = (nbytes_per_part + GRAN - 1) // GRAN
        base = self.ar_cur
        self.ar_cur += g
        self.ar_high = max(self.ar_high, self.ar_cur)
        assert self.ar_cur <= self.ar_ng, \
            f"arena overflow: {self.ar_cur}/{self.ar_ng} granules"
        return base

    def ar_mark(self):
        self.ar_marks.append(self.ar_cur)

    def ar_release(self):
        self.ar_cur = self.ar_marks.pop()

    def fconst(self, v):
        v = float(v)
        if v not in self.fc_map:
            i = len(self.fc_map)
            assert i < 16
            self.fc_map[v] = i
            self.p.op("dve", lambda ve, _i=i, _v=v: ve.memset(
                self.fc[:, _i:_i + 1], _v), writes=[("fconsts", i)], tag="fc")
        i = self.fc_map[v]
        return self.fc[:, i:i + 1], ("fconsts", i)

    def nuid(self):
        self.uid += 1
        return self.uid

    def dram_scratch(self, name, shape, dtype):
        return self.nc.dram_tensor(name, list(shape), dtype, kind="Internal").ap()

    def inp(self, name):
        if name in self._dram:
            return self._dram[name]
        arr = self.host[name]
        ap = self.nc.dram_tensor(name, list(arr.shape), F32,
                                 kind="ExternalInput").ap()
        self._dram[name] = ap
        self.inputs_used.add(name)
        return ap

    def const_sb(self, name, arr, dtype):
        self.host[name] = np.ascontiguousarray(arr.astype(np.float32))
        ap = self.inp(name)
        t = self.p.sbuf(name + "_sb", list(arr.shape), dtype)
        q = "sp" if dtype == F32 else "pool"
        self.p.dma(q, t[:], ap, writes=[(name + "_sb", None)])
        return t

    def wtile(self):
        i = self.wb_rr % len(self.wb)
        self.wb_rr += 1
        return self.wb[i], f"wb{i}"

    def load_vec_fm(self, name, dram_ap_1d, n):
        p = self.p
        cols = (n + 127) // 128
        dst = p.sbuf(name, [128, cols], F32)
        nfull = n // 128
        bank, bkey = self.psb[7], self.pskey[7]
        if nfull:
            for c0 in range(0, nfull, 32):
                cn = min(32, nfull - c0)
                p.dma("sp", self.vstg_sh[0:cn, :],
                      dram_ap_1d[c0 * 128:(c0 + cn) * 128].rearrange(
                          "(c p) -> c p", p=128),
                      writes=[("vstg_sh", None)])
                p.op("pe", lambda te, _cn=cn: te.transpose(
                    bank[:, 0:_cn], self.vstg_sh[0:_cn, :],
                    self.ident_f[0:_cn, 0:_cn]),
                    reads=[("vstg_sh", None), ("c_ident_sb", None)],
                    writes=[(bkey, None)], tag="vtr")
                p.op("dve", lambda ve, _c0=c0, _cn=cn: ve.tensor_copy(
                    dst[:, _c0:_c0 + _cn], bank[:, 0:_cn]),
                    reads=[(bkey, None)], writes=[(name, None)], tag="vcp")
        rem = n - nfull * 128
        if rem:
            p.dma("sp", self.vstg2_sh[0:1, 0:rem],
                  dram_ap_1d[nfull * 128:n].rearrange("(c p) -> c p", p=rem),
                  writes=[("vstg2_sh", None)])
            p.op("pe", lambda te: te.transpose(bank[0:rem, 480:481],
                                               self.vstg2_sh[0:1, 0:rem],
                                               self.ident_f[0:1, 0:1]),
                 reads=[("vstg2_sh", None), ("c_ident_sb", None)],
                 writes=[(bkey, None)], tag="vtr2")
            p.op("dve", lambda ve: ve.tensor_copy(dst[0:rem, nfull:nfull + 1],
                                                  bank[0:rem, 480:481]),
                 reads=[(bkey, None)], writes=[(name, None)], tag="vcp2")
        return dst


def dk(dram_ap, t=None):
    return (dram_ap.tensor.name, t)


def emit_ln(ctx, src_dram, dst, t0, t1, tp=512, eps=1e-5, dst32=None,
            dst_t0=None, gcol=None, bcol=None, affkeys=(), route=None):
    p = ctx.p
    if dst_t0 is None:
        dst_t0 = t0
    u = ctx.nuid()
    ctx.ar_mark()
    xf = Buf(ctx, NCH, tp, F32)
    sq = Buf(ctx, 1, tp, F32)
    nrm = Buf(ctx, 1, tp, F32) if route is not None else None
    rows = ctx.rows
    RK = "rows"
    for t in range(t0, t1, tp):
        tl = min(tp, t1 - t)
        bsum, bsq = ctx.psb[6], ctx.psb[7]
        for c in range(NCH):
            p.dma("act", xf.ap(c, 0, tl), src_dram[c * 128:(c + 1) * 128, t:t + tl],
                  reads=[dk(src_dram, t // 512)], writes=xf.keys(c), tag="lnld")
            p.op("pe", lambda te, _c=c, _tl=tl: te.matmul(
                bsum[0:1, 0:_tl], ctx.ones_f[:, 0:1], xf.ap(_c, 0, _tl),
                start=(_c == 0), stop=(_c == NCH - 1)),
                reads=xf.keys(c) + [("c_ones_sb", None)],
                writes=[(ctx.pskey[6], None)], tag="lnsum")
            p.op("act", lambda sc, _c=c, _tl=tl: sc.activation(
                sq.ap(0, 0, _tl), xf.ap(_c, 0, _tl), AF.Square),
                reads=xf.keys(c), writes=sq.keys(0), tag="lnsq")
            p.op("pe", lambda te, _c=c, _tl=tl: te.matmul(
                bsq[0:1, 0:_tl], ctx.ones_f[:, 0:1], sq.ap(0, 0, _tl),
                start=(_c == 0), stop=(_c == NCH - 1)),
                reads=sq.keys(0) + [("c_ones_sb", None)],
                writes=[(ctx.pskey[7], None)], tag="lnssq")
        p.op("dve", lambda ve, _tl=tl: ve.tensor_scalar_mul(
            rows[:, 0:_tl], bsum[0:1, 0:_tl], 1.0 / D),
            reads=[(ctx.pskey[6], None)], writes=[(RK, 0)], tag="lnr0")
        p.op("dve", lambda ve, _tl=tl: ve.tensor_scalar_mul(
            rows[:, 512:512 + _tl], bsq[0:1, 0:_tl], 1.0 / D),
            reads=[(ctx.pskey[7], None)], writes=[(RK, 1)], tag="lnr1")
        p.op("dve", lambda ve, _tl=tl: ve.scalar_tensor_tensor(
            rows[:, 1024:1024 + _tl], rows[:, 0:_tl], -1.0, rows[:, 0:_tl],
            ALU.mult, ALU.mult),
            reads=[(RK, 0)], writes=[(RK, 2)], tag="lnm2")
        p.op("dve", lambda ve, _tl=tl: ve.tensor_add(
            rows[:, 512:512 + _tl], rows[:, 512:512 + _tl],
            rows[:, 1024:1024 + _tl]),
            reads=[(RK, 1), (RK, 2)], writes=[(RK, 1)], tag="lnvar")
        eap, ek = ctx.fconst(eps)
        p.op("act", lambda sc, _tl=tl, _e=eap: sc.activation(
            rows[:, 512:512 + _tl], rows[:, 512:512 + _tl], AF.Ln,
            bias=_e[0:1, 0:1]),
            reads=[(RK, 1), ek], writes=[(RK, 1)], tag="lnln")
        p.op("act", lambda sc, _tl=tl: sc.activation(
            rows[:, 512:512 + _tl], rows[:, 512:512 + _tl], AF.Exp, scale=-0.5),
            reads=[(RK, 1)], writes=[(RK, 1)], tag="lnex")
        bm, br = ctx.psb[4], ctx.psb[5]
        p.op("pe", lambda te, _tl=tl: te.matmul(
            bm[:, 0:_tl], ctx.ones_f[0:1, :], rows[:, 0:_tl],
            start=True, stop=True),
            reads=[(RK, 0), ("c_ones_sb", None)],
            writes=[(ctx.pskey[4], None)], tag="lnbm")
        p.op("pe", lambda te, _tl=tl: te.matmul(
            br[:, 0:_tl], ctx.ones_f[0:1, :], rows[:, 512:512 + _tl],
            start=True, stop=True),
            reads=[(RK, 1), ("c_ones_sb", None)],
            writes=[(ctx.pskey[5], None)], tag="lnbr")
        td = t - dst_t0
        for c in range(NCH):
            p.op("dve", lambda ve, _c=c, _tl=tl: ve.tensor_sub(
                sq.ap(0, 0, _tl), xf.ap(_c, 0, _tl), bm[:, 0:_tl]),
                reads=xf.keys(c) + [(ctx.pskey[4], None)],
                writes=sq.keys(0), tag="lnsub")
            if route is not None:
                gsb, gkeys, lbuf, lt0 = route
                tgt = nrm
                p.op("dve", lambda ve, _c=c, _tl=tl: ve.tensor_mul(
                    tgt.ap(0, 0, _tl), sq.ap(0, 0, _tl), br[:, 0:_tl]),
                    reads=sq.keys(0) + [(ctx.pskey[5], None)],
                    writes=tgt.keys(0), tag="lnrm")
                if gcol is not None:
                    p.op("dve", lambda ve, _c=c, _tl=tl: ve.tensor_scalar(
                        tgt.ap(0, 0, _tl), tgt.ap(0, 0, _tl),
                        gcol[:, _c:_c + 1], bcol[:, _c:_c + 1],
                        ALU.mult, ALU.add),
                        reads=tgt.keys(0), sreads=list(affkeys),
                        writes=tgt.keys(0), tag="lnrma")
                for sub in range(tl // 128):
                    bank = ctx.psb[sub]
                    bkey = ctx.pskey[sub]
                    p.op("pe", lambda te, _c=c, _s=sub, _f=(c == 0),
                         _l=(c == NCH - 1): te.matmul(
                             bank[0:128, 0:8] if False else ctx.psb[_s][0:128, 0:8],
                             tgt.ap(0, _s * 128, 128), gsb[:, _c * 8:(_c + 1) * 8],
                             start=_f, stop=_l),
                         reads=tgt.keys(0, sub * 128, 128) + list(gkeys),
                         writes=[(bkey, None)], tag="lnrt")
                p.op("act", lambda sc, _c=c, _tl=tl, _td=td: sc.activation(
                    dst.ap(_c, _td, _tl), tgt.ap(0, 0, _tl), AF.Copy),
                    reads=tgt.keys(0), writes=dst.keys(c, td, tl),
                    tag="lncast2")
                continue
            if dst32 is not None:
                p.op("dve", lambda ve, _c=c, _tl=tl, _td=td: ve.tensor_mul(
                    dst32.ap(_c, _td, _tl), sq.ap(0, 0, _tl), br[:, 0:_tl]),
                    reads=sq.keys(0) + [(ctx.pskey[5], None)],
                    writes=dst32.keys(c, td, tl), tag="lnm32")
                if dst is not None:
                    p.op("act", lambda sc, _c=c, _tl=tl, _td=td: sc.activation(
                        dst.ap(_c, _td, _tl), dst32.ap(_c, _td, _tl), AF.Copy),
                        reads=dst32.keys(c, td, tl),
                        writes=dst.keys(c, td, tl), tag="lncast")
            elif gcol is not None:
                p.op("dve", lambda ve, _c=c, _tl=tl: ve.tensor_mul(
                    sq.ap(0, 0, _tl), sq.ap(0, 0, _tl), br[:, 0:_tl]),
                    reads=sq.keys(0) + [(ctx.pskey[5], None)],
                    writes=sq.keys(0), tag="lnmulA")
                p.op("dve", lambda ve, _c=c, _tl=tl, _td=td: ve.tensor_scalar(
                    dst.ap(_c, _td, _tl), sq.ap(0, 0, _tl),
                    gcol[:, _c:_c + 1], bcol[:, _c:_c + 1],
                    ALU.mult, ALU.add),
                    reads=sq.keys(0), sreads=list(affkeys),
                    writes=dst.keys(c, td, tl), tag="lnaff")
            else:
                p.op("dve", lambda ve, _c=c, _tl=tl, _td=td: ve.tensor_mul(
                    dst.ap(_c, _td, _tl), sq.ap(0, 0, _tl), br[:, 0:_tl]),
                    reads=sq.keys(0) + [(ctx.pskey[5], None)],
                    writes=dst.keys(c, td, tl), tag="lnmul")
        if route is not None:
            gsb, gkeys, lbuf, lt0 = route
            for sub in range(tl // 128):
                lts = (t - dst_t0) // 128 + sub - lt0
                p.op("act", lambda sc, _s=sub, _lts=lts: sc.activation(
                    lbuf.ap(_lts, 0, 8), ctx.psb[_s][0:128, 0:8], AF.Copy),
                    reads=[(ctx.pskey[sub], None)],
                    writes=lbuf.keys(lts, 0, 8), tag="lnrte")
    ctx.ar_release()


def linear_fm(ctx, tag, xfn, IN, OUT, T, wsrc, sink, tp=512, tg=2):
    p = ctx.p
    kcs = [(k0, min(128, IN - k0)) for k0 in range(0, IN, 128)]
    SEGK = 8
    for tb in range(0, T, tp * tg):
        ntg = min(tg, (T - tb + tp - 1) // tp)
        for c0 in range(0, OUT, 512):
            cl = min(512, OUT - c0)
            dchunks = [(d0, min(128, cl - d0)) for d0 in range(0, cl, 128)]
            assert len(dchunks) * ntg <= 8
            nseg = (len(kcs) + SEGK - 1) // SEGK
            for si0 in range(nseg):
                seg = kcs[si0 * SEGK:(si0 + 1) * SEGK]
                tiles = []
                for (k0, klen) in seg:
                    wt, wkey = ctx.wtile()
                    src, skeys = wsrc(k0, klen, c0, cl)
                    p.dma("pool", wt[0:klen, 0:cl], src, reads=skeys,
                          writes=[(wkey, None)], tag=tag + "w")
                    tiles.append((k0, klen, wt, wkey))
                for di, (d0, dl) in enumerate(dchunks):
                    for g in range(ntg):
                        t = tb + g * tp
                        tl = min(tp, T - t)
                        bank = ctx.psb[di * tg + g]
                        bkey = ctx.pskey[di * tg + g]
                        for si, (k0, klen, wt, wkey) in enumerate(tiles):
                            first = (si0 == 0 and si == 0)
                            last = (si0 == nseg - 1 and si == len(tiles) - 1)
                            xap, xkeys = xfn(k0, klen, t, tl)
                            p.op("pe", lambda te, _b=bank, _w=wt, _kl=klen,
                                 _d0=d0, _dl=dl, _x=xap, _tl=tl,
                                 _f=first, _l=last: te.matmul(
                                     _b[0:_dl, 0:_tl], _w[0:_kl, _d0:_d0 + _dl],
                                     _x, start=_f, stop=_l),
                                 reads=[(wkey, None)] + xkeys,
                                 writes=[(bkey, None)], tag=tag + "mm")
                        if si0 == nseg - 1:
                            sink(bank, bkey, (c0 + d0) // 128, dl, t, tl)


def xfn_buf(buf, toff=0):
    def f(k0, klen, t, tl):
        c = k0 // 128
        return buf.ap(c, toff + t, tl), buf.keys(c, toff + t, tl)
    return f


def wsrc_2d(ctx, name):
    ap = ctx.inp(name)

    def f(k0, klen, c0, cl):
        return ap[k0:k0 + klen, c0:c0 + cl], []
    return f


def sink_to_buf(ctx, tag, dst, bias_sb=None, bias_key=None, act=None, toff=0):
    p = ctx.p

    def sink(bank, bkey, cc, dl, t, tl):
        kwargs = {}
        rd = [(bkey, None)]
        if bias_sb is not None:
            kwargs["bias"] = bias_sb[0:dl, cc:cc + 1]
            rd.append((bias_key, None))
        fn = act or (AF.Identity if bias_sb is not None else AF.Copy)
        p.op("act", lambda sc, _b=bank, _cc=cc, _dl=dl, _t=t, _tl=tl: sc.activation(
            dst.ap(_cc, toff + _t, _tl)[0:_dl], _b[0:_dl, 0:_tl], fn, **kwargs),
            reads=rd, writes=dst.keys(cc, toff + t, tl), tag=tag + "ev")
    return sink


def sink_residual(ctx, tag, src_dram, dst_dram, bias_sb=None, bias_key=None,
                  toff=0):
    p = ctx.p
    tmp = Buf(ctx, 1, 512, F32)
    xr = Buf(ctx, 1, 512, F32)
    res = Buf(ctx, 1, 512, F32)

    def sink(bank, bkey, cc, dl, t, tl):
        t_g = toff + t
        kwargs = {}
        rd = [(bkey, None)]
        if bias_sb is not None:
            kwargs["bias"] = bias_sb[:, cc:cc + 1]
            rd.append((bias_key, None))
        _fn = AF.Identity if bias_sb is not None else AF.Copy
        p.op("act", lambda sc, _b=bank, _dl=dl, _tl=tl: sc.activation(
            tmp.ap(0, 0, _tl), _b[0:_dl, 0:_tl], _fn, **kwargs),
            reads=rd, writes=tmp.keys(0), tag=tag + "ev")
        p.dma("act", xr.ap(0, 0, tl)[0:dl],
              src_dram[cc * 128:cc * 128 + dl, t_g:t_g + tl],
              reads=[dk(src_dram, t_g // 512)], writes=xr.keys(0), tag=tag + "xl")
        p.op("dve", lambda ve, _dl=dl, _tl=tl: ve.tensor_add(
            res.ap(0, 0, _tl)[0:_dl], tmp.ap(0, 0, _tl)[0:_dl],
            xr.ap(0, 0, _tl)[0:_dl]),
            reads=tmp.keys(0) + xr.keys(0), writes=res.keys(0), tag=tag + "add")
        p.dma("sp", dst_dram[cc * 128:cc * 128 + dl, t_g:t_g + tl],
              res.ap(0, 0, tl)[0:dl],
              reads=res.keys(0), writes=[dk(dst_dram, t_g // 512)], tag=tag + "xs")
    return sink


def emit_l0a(ctx, src, dst):
    p = ctx.p
    bo = ctx.load_vec_fm("l0a_bo", ctx.inp("L0a_bo"), D)
    wq = wsrc_2d(ctx, "L0a_WqT")
    wk = wsrc_2d(ctx, "L0a_WkT")
    wv = wsrc_2d(ctx, "L0a_WvT")
    wo = wsrc_2d(ctx, "L0a_WoT")

    ctx.ar_mark()
    xn = Buf(ctx, NCH, SEQ, BF16)
    ktm = Buf(ctx, SEQ // 128, D, BF16)
    vtm = Buf(ctx, SEQ // 128, D + 16, BF16)
    qp = Buf(ctx, 1, SEQ, BF16)
    cxf = Buf(ctx, NCH, SEQ, BF16)
    kvb = p.sbuf("l0a_kv", [128, 8 * 132], F32)
    kvbd = p.sbuf("l0a_kbd", [128, 128], BF16)
    p.hot_names.add("l0a_kbd")
    p.hot_names.add("l0a_ks2")
    p.hot_names.add("zrow2")
    ks2 = p.sbuf("l0a_ks2", [128, 2], BF16)
    elu = Buf(ctx, 1, 512, F32)
    pvs = Buf(ctx, 1, 512, F32)

    def elu_feature(bank, bkey, dst_ap, dst_keys, tl):
        p.op("dve", lambda ve, _b=bank, _tl=tl: ve.tensor_scalar_min(
            elu.ap(0, 0, _tl), _b[:, 0:_tl], 0.0),
            reads=[(bkey, None)], writes=elu.keys(0), tag="l0a_min")
        p.op("act", lambda sc, _tl=tl: sc.activation(
            elu.ap(0, 0, _tl), elu.ap(0, 0, _tl), AF.Exp),
            reads=elu.keys(0), writes=elu.keys(0), tag="l0a_exp")
        p.op("dve", lambda ve, _b=bank, _tl=tl, _d=dst_ap: ve.scalar_tensor_tensor(
            _d, _b[:, 0:_tl], 0.0, elu.ap(0, 0, _tl), ALU.max, ALU.add),
            reads=[(bkey, None)] + elu.keys(0), writes=dst_keys, tag="l0a_stt")

    for s in range(2):
        st = s * SEQ
        emit_ln(ctx, src, xn, st, st + SEQ, dst_t0=st)
        xf = xfn_buf(xn)

        def make_tm(wacc, dstb, feat):
            for i in range(SEQ // 128):
                t = i * 128
                for half in range(2):
                    bank = ctx.psb[half]
                    bkey = ctx.pskey[half]
                    for k in range(NCH):
                        wt, wkey = ctx.wtile()
                        src_ap, _ = wacc(k * 128, 128, half * 512, 512)
                        p.dma("pool", wt[:, 0:512], src_ap,
                              writes=[(wkey, None)], tag="l0a_wtm")
                        p.op("pe", lambda te, _b=bank, _w=wt, _k=k, _t=t,
                             _f=(k == 0), _l=(k == NCH - 1): te.matmul(
                                 _b[:], xn.ap(_k, _t, 128), _w[:, 0:512],
                                 start=_f, stop=_l),
                             reads=[(wkey, None)] + xn.keys(k, t, 128),
                             writes=[(bkey, None)], tag="l0a_tmm")
                    dap = dstb.ap(i, half * 512, 512)
                    dkeys = dstb.keys(i, half * 512, 512)
                    if feat:
                        elu_feature(bank, bkey, dap, dkeys, 512)
                    else:
                        p.op("act", lambda sc, _b=bank, _d=dap: sc.activation(
                            _d, _b[:], AF.Copy),
                            reads=[(bkey, None)], writes=dkeys, tag="l0a_vev")

        make_tm(wk, ktm, True)
        make_tm(wv, vtm, False)
        for i in range(SEQ // 128):
            p.op("dve", lambda ve, _i=i: ve.memset(vtm.ap(_i, D, 1), 1.0),
                 writes=vtm.keys(i, D, 1), tag="l0a_one")

        for pr in range(8):
            bank = ctx.psb[2 + (pr % 2)]
            bkey = ctx.pskey[2 + (pr % 2)]
            for i in range(SEQ // 128):
                p.op("pe", lambda te, _b=bank, _i=i, _pr=pr, _f=(i == 0): te.matmul(
                    _b[:, 0:128], ktm.ap(_i, _pr * 128, 128),
                    vtm.ap(_i, _pr * 128, 128), start=_f, stop=False),
                    reads=ktm.keys(i, pr * 128, 128) + vtm.keys(i, pr * 128, 128),
                    writes=[(bkey, None)], tag="l0a_kv")
                p.op("pe", lambda te, _b=bank, _i=i, _pr=pr,
                     _l=(i == SEQ // 128 - 1): te.matmul(
                         _b[:, 128:129], ktm.ap(_i, _pr * 128, 128),
                         vtm.ap(_i, D, 1), start=False, stop=_l),
                     reads=ktm.keys(i, pr * 128, 128) + vtm.keys(i, D, 1),
                     writes=[(bkey, None)], tag="l0a_ks")
            p.op("act", lambda sc, _b=bank, _pr=pr: sc.activation(
                kvb[:, _pr * 132:_pr * 132 + 129], _b[:, 0:129], AF.Copy),
                reads=[(bkey, None)], writes=[("l0a_kv", pr)], tag="l0a_kvev")

        for pr in range(8):
            qsink_done = []

            def qsink(bank, bkey, cc, dl, t, tl):
                elu_feature(bank, bkey, qp.ap(0, t, tl), qp.keys(0, t, tl), tl)

            linear_fm(ctx, "l0a_q", xf, D, 128, SEQ,
                      lambda k0, kl, c0, cl, _pr=pr: wq(k0, kl, _pr * 128 + c0, cl),
                      qsink, tp=512, tg=2)
            p.op("dve", lambda ve: ve.memset(kvbd[:], 0.0),
                 writes=[("l0a_kbd", None)], tag="l0a_b0")
            p.op("dve", lambda ve, _pr=pr: ve.tensor_copy(
                kvbd[0:64, 0:64], kvb[0:64, _pr * 132:_pr * 132 + 64]),
                reads=[("l0a_kv", pr)], writes=[("l0a_kbd", None)], tag="l0a_bA")
            p.op("dve", lambda ve, _pr=pr: ve.tensor_copy(
                kvbd[64:128, 64:128],
                kvb[64:128, _pr * 132 + 64:_pr * 132 + 128]),
                reads=[("l0a_kv", pr)], writes=[("l0a_kbd", None)], tag="l0a_bB")
            p.op("dve", lambda ve: ve.memset(ks2[:], 0.0),
                 writes=[("l0a_ks2", None)], tag="l0a_s0")
            p.op("dve", lambda ve, _pr=pr: ve.tensor_copy(
                ks2[0:64, 0:1], kvb[0:64, _pr * 132 + 128:_pr * 132 + 129]),
                reads=[("l0a_kv", pr)], writes=[("l0a_ks2", None)], tag="l0a_sA")
            p.op("dve", lambda ve, _pr=pr: ve.tensor_copy(
                ks2[64:128, 1:2], kvb[64:128, _pr * 132 + 128:_pr * 132 + 129]),
                reads=[("l0a_kv", pr)], writes=[("l0a_ks2", None)], tag="l0a_sB")
            for t in range(0, SEQ, 512):
                bank = ctx.psb[4 + (t // 512) % 2]
                bkey = ctx.pskey[4 + (t // 512) % 2]
                zb, zbk = ctx.psb[6], ctx.pskey[6]
                zbb, zbbk = ctx.psb[7], ctx.pskey[7]
                p.op("pe", lambda te, _b=bank, _t=t: te.matmul(
                    _b[:], kvbd[:], qp.ap(0, _t, 512), start=True, stop=True),
                    reads=[("l0a_kbd", None)] + qp.keys(0, t, 512),
                    writes=[(bkey, None)], tag="l0a_cx")
                p.op("pe", lambda te, _b=zb, _t=t: te.matmul(
                    _b[0:2, 0:512], ks2[:], qp.ap(0, _t, 512),
                    start=True, stop=True),
                    reads=[("l0a_ks2", None)] + qp.keys(0, t, 512),
                    writes=[(zbk, None)], tag="l0a_z")
                eap6, ek6 = ctx.fconst(1e-6)
                p.op("act", lambda sc, _b=zb, _e=eap6: sc.activation(
                    ctx.zrow[:], _b[0:2, 0:512], AF.Ln, bias=_e[0:2, 0:1]),
                    reads=[(zbk, None), ek6], writes=[("zrow2", None)], tag="l0a_zl")
                p.op("act", lambda sc: sc.activation(
                    ctx.zrow[:], ctx.zrow[:], AF.Exp, scale=-1.0),
                    reads=[("zrow2", None)], writes=[("zrow2", None)],
                    tag="l0a_ze")
                p.op("pe", lambda te, _b=zbb: te.matmul(
                    _b[:], ctx.hsel[:], ctx.zrow[:], start=True, stop=True),
                    reads=[("zrow2", None), ("c_hsel_sb", None)],
                    writes=[(zbbk, None)], tag="l0a_zb")
                p.op("act", lambda sc, _b=bank: sc.activation(
                    pvs.ap(0), _b[:], AF.Copy),
                    reads=[(bkey, None)], writes=pvs.keys(0), tag="l0a_ce")
                p.op("dve", lambda ve, _pr=pr, _t=t, _b=zbb: ve.tensor_mul(
                    cxf.ap(_pr, _t, 512), pvs.ap(0), _b[:]),
                    reads=pvs.keys(0) + [(zbbk, None)],
                    writes=cxf.keys(pr, t, 512), tag="l0a_dv")

        rsink = sink_residual(ctx, "l0a_o", src, dst, bias_sb=bo,
                              bias_key="l0a_bo", toff=st)
        linear_fm(ctx, "l0a_o", xfn_buf(cxf), D, D, SEQ, wo, rsink,
                  tp=512, tg=2)
    ctx.ar_release()



ATTN_DBG = set()


def attn_softmax(ctx, tag, contribs, vtm, cxf, nheads=16, T=SEQ, causal=True,
                 scale=0.125):
    p = ctx.p
    nkt = T // 128
    ctx.ar_mark()
    PT = Buf(ctx, nkt, 512, BF16)
    mcols = Buf(ctx, 1, 8, F32, hot=True)
    nm = Buf(ctx, 1, 512, BF16)
    pvs = Buf(ctx, 1, 512, F32)
    rz = Buf(ctx, 1, 512, F32)
    tri01 = ctx.tri01
    for h in range(nheads):
        for qb in range(0, T, 512):
            kmax = qb + 512 if causal else T
            for j in range(4):
                qs = qb + j * 128
                sb_, sbk = ctx.psb[j % 2], ctx.pskey[j % 2]
                for kb in range(0, kmax, 512):
                    kl = min(512, kmax - kb)
                    ctrA = [cc for i2, cc in enumerate(contribs)
                            if i2 == 0 or "noA" not in ATTN_DBG]
                    nctr = len(ctrA)
                    for ci, (qfn, kfn, dim) in enumerate(ctrA):
                        qap, qk = qfn(h, qs, 128)
                        kap, kk = kfn(h, kb, kl)
                        p.op("pe", lambda te, _s=sb_, _q=qap, _k=kap, _kl=kl,
                             _f=(ci == 0), _l=(ci == nctr - 1): te.matmul(
                                 _s[:, 0:_kl], _q, _k, start=_f, stop=_l),
                             reads=qk + kk, writes=[(sbk, None)], tag=tag + "sA")
                    if kb == 0:
                        p.op("dve", lambda ve, _s=sb_, _j=j, _kl=kl: ve.reduce_max(
                            mcols.ap(0, _j, 1), _s[:, 0:_kl], axis=AX.X),
                            reads=[(sbk, None)], writes=mcols.keys(0, j, 1),
                            tag=tag + "mx")
                    else:
                        p.op("dve", lambda ve, _s=sb_, _kl=kl: ve.reduce_max(
                            mcols.ap(0, 7, 1), _s[:, 0:_kl], axis=AX.X),
                            reads=[(sbk, None)], writes=mcols.keys(0),
                            tag=tag + "mx2")
                        p.op("dve", lambda ve, _j=j: ve.tensor_max(
                            mcols.ap(0, _j, 1), mcols.ap(0, _j, 1),
                            mcols.ap(0, 7, 1)),
                            reads=mcols.keys(0),
                            writes=mcols.keys(0), tag=tag + "mx3")
            nmb, nmbk = ctx.psb[2], ctx.pskey[2]
            for j in range(4):
                p.op("pe", lambda te, _j=j: te.matmul(
                    nmb[0:1, _j * 128:(_j + 1) * 128], mcols.ap(0, _j, 1),
                    ctx.nident_f[:], start=True, stop=True),
                    reads=mcols.keys(0, j, 1) + [("c_nident_sb", None)],
                    writes=[(nmbk, None)], tag=tag + "nm")
            p.op("act", lambda sc: sc.activation(
                nm.ap(0, 0, 512)[0:1], nmb[0:1, 0:512], AF.Copy),
                reads=[(nmbk, None)], writes=nm.keys(0), tag=tag + "nme")
            for kt in range(0, kmax, 128):
                i = kt // 128
                stb, stbk = ctx.psb[3 + (i % 2)], ctx.pskey[3 + (i % 2)]
                ctrB = [cc for i2, cc in enumerate(contribs)
                        if i2 == 0 or "noB" not in ATTN_DBG]
                nctr = len(ctrB)
                for ci, (qfn, kfn, dim) in enumerate(ctrB):
                    qap, qk = qfn(h, qb, 512)
                    kap, kk = kfn(h, kt, 128)
                    p.op("pe", lambda te, _s=stb, _q=qap, _k=kap,
                         _f=(ci == 0): te.matmul(
                             _s[:], _k, _q, start=_f, stop=False),
                         reads=qk + kk, writes=[(stbk, None)], tag=tag + "sT")
                p.op("pe", lambda te, _s=stb: te.matmul(
                    _s[:], ctx.ones_b[0:1, :], nm.ap(0, 0, 512)[0:1],
                    start=False, stop=True),
                    reads=nm.keys(0) + [("c_onesb_sb", None)],
                    writes=[(stbk, None)], tag=tag + "r1")
                p.op("act", lambda sc, _s=stb, _i=i: sc.activation(
                    PT.ap(_i), _s[:], AF.Exp, scale=float(scale)),
                    reads=[(stbk, None)], writes=PT.keys(i), tag=tag + "ex")
                if causal:
                    for j in range(4):
                        q0 = qb + j * 128
                        if kt == q0:
                            p.op("dve", lambda ve, _i=i, _j=j: ve.tensor_mul(
                                PT.ap(_i, _j * 128, 128), PT.ap(_i, _j * 128, 128),
                                tri01[:]),
                                reads=PT.keys(i, j * 128, 128) + [("c_tri01_sb", None)],
                                writes=PT.keys(i, j * 128, 128), tag=tag + "msk")
                        elif kt > q0:
                            p.op("dve", lambda ve, _i=i, _j=j: ve.memset(
                                PT.ap(_i, _j * 128, 128), 0.0),
                                writes=PT.keys(i, j * 128, 128), tag=tag + "z0")
            pvb, pvbk = ctx.psb[5], ctx.pskey[5]
            zb, zbk = ctx.psb[6], ctx.pskey[6]
            bcb, bcbk = ctx.psb[7], ctx.pskey[7]
            nk = kmax // 128
            for i in range(nk):
                p.op("pe", lambda te, _i=i, _h=h, _f=(i == 0),
                     _l=(i == nk - 1): te.matmul(
                         pvb[0:64, 0:512], vtm.ap(_i, _h * 64, 64),
                         PT.ap(_i), start=_f, stop=_l),
                     reads=vtm.keys(i, h * 64, 64) + PT.keys(i),
                     writes=[(pvbk, None)], tag=tag + "pv")
                p.op("pe", lambda te, _i=i, _f=(i == 0),
                     _l=(i == nk - 1): te.matmul(
                         zb[0:1, 0:512], ctx.ones_b[:, 0:1], PT.ap(_i),
                         start=_f, stop=_l),
                     reads=PT.keys(i) + [("c_onesb_sb", None)],
                     writes=[(zbk, None)], tag=tag + "zz")
            p.op("act", lambda sc: sc.activation(
                rz.ap(0, 0, 512)[0:1], zb[0:1, 0:512], AF.Ln),
                reads=[(zbk, None)], writes=rz.keys(0), tag=tag + "zl")
            p.op("act", lambda sc: sc.activation(
                rz.ap(0, 0, 512)[0:1], rz.ap(0, 0, 512)[0:1], AF.Exp,
                scale=-1.0),
                reads=rz.keys(0), writes=rz.keys(0), tag=tag + "ze")
            p.op("pe", lambda te: te.matmul(
                bcb[0:64, 0:512], ctx.ones_f[0:1, 0:64],
                rz.ap(0, 0, 512)[0:1], start=True, stop=True),
                reads=rz.keys(0) + [("c_ones_sb", None)],
                writes=[(bcbk, None)], tag=tag + "zb")
            p.op("act", lambda sc: sc.activation(
                pvs.ap(0, 0, 512)[0:64], pvb[0:64, 0:512], AF.Copy),
                reads=[(pvbk, None)], writes=pvs.keys(0), tag=tag + "pe")
            co = (h % 2) * 64
            p.op("dve", lambda ve, _h=h, _qb=qb, _co=co: ve.tensor_mul(
                cxf.ap(_h // 2, _qb, 512)[_co:_co + 64],
                pvs.ap(0, 0, 512)[0:64], bcb[0:64, 0:512]),
                reads=pvs.keys(0) + [(bcbk, None)],
                writes=cxf.keys(h // 2, qb, 512), tag=tag + "dv")
    ctx.ar_release()


def make_vtm(ctx, tag, xacc, wsrc, vtm, T=SEQ, IN=D, bias_row=None,
             bias_keys=(), HALFW=512):
    p = ctx.p
    for i in range(T // 128):
        t = i * 128
        for half in range(2):
            bank, bkey = ctx.psb[half], ctx.pskey[half]
            nk = (IN + 127) // 128
            for k in range(nk):
                kl = min(128, IN - k * 128)
                wt, wkey = ctx.wtile()
                src_ap, skeys = wsrc(k * 128, kl, half * HALFW, HALFW)
                p.dma("pool", wt[0:kl, 0:HALFW], src_ap, reads=skeys,
                      writes=[(wkey, None)], tag=tag + "w")
                xap, xkeys = xacc(k * 128, kl, t, 128)
                p.op("pe", lambda te, _b=bank, _w=wt, _x=xap, _k=k, _kl=kl,
                     _f=(k == 0): te.matmul(
                         _b[:, 0:HALFW], _x, _w[0:_kl, 0:HALFW], start=_f,
                         stop=(False if bias_row is not None else _k == nk - 1)),
                     reads=[(wkey, None)] + xkeys, writes=[(bkey, None)],
                     tag=tag + "mm")
            if bias_row is not None:
                p.op("pe", lambda te, _b=bank, _h=half: te.matmul(
                    _b[:, 0:HALFW], ctx.ones_b[0:1, :],
                    bias_row[0:1, _h * HALFW:(_h + 1) * HALFW],
                    start=False, stop=True),
                    reads=list(bias_keys) + [("c_onesb_sb", None)],
                    writes=[(bkey, None)], tag=tag + "bb")
            p.op("act", lambda sc, _b=bank, _i=i, _h=half: sc.activation(
                vtm.ap(_i, _h * HALFW, HALFW), _b[:, 0:HALFW], AF.Copy),
                reads=[(bkey, None)], writes=vtm.keys(i, half * HALFW, HALFW),
                tag=tag + "ev")


def emit_l2a(ctx, src, dst):
    p = ctx.p
    wkt = wsrc_2d(ctx, "L2a_WKT")
    wqt = wsrc_2d(ctx, "L2a_WQT")
    wvt = wsrc_2d(ctx, "L2a_WVT")
    wot = wsrc_2d(ctx, "L2a_WOT")
    for s in range(2):
        st = s * SEQ
        ctx.ar_mark()
        vtm = Buf(ctx, SEQ // 128, D, BF16)
        qfm = Buf(ctx, NCH, SEQ, BF16)
        kfm = Buf(ctx, NCH, SEQ, BF16)
        ctx.ar_mark()
        xn = Buf(ctx, NCH, SEQ, BF16)
        emit_ln(ctx, src, xn, st, st + SEQ, dst_t0=st)
        xf = xfn_buf(xn)
        make_vtm(ctx, "l2a_v", xf, wvt, vtm)
        linear_fm(ctx, "l2a_q", xf, D, D, SEQ, wqt,
                  sink_to_buf(ctx, "l2a_q", qfm), tp=512, tg=2)
        linear_fm(ctx, "l2a_k", xf, D, D, SEQ, wkt,
                  sink_to_buf(ctx, "l2a_k", kfm), tp=512, tg=2)
        ctx.ar_release()
        cxf = Buf(ctx, NCH, SEQ, BF16)

        def hacc(buf):
            def f(h, t0, tl):
                c, off = h // 2, (h % 2) * 64
                return buf.ap(c, t0, tl)[off:off + 64], buf.keys(c, t0, tl)
            return f

        attn_softmax(ctx, "l2a", [(hacc(qfm), hacc(kfm), 64)], vtm, cxf,
                     causal=True, scale=0.125)
        rsink = sink_residual(ctx, "l2a_o", src, dst, toff=st)
        linear_fm(ctx, "l2a_o", xfn_buf(cxf), D, D, SEQ, wot, rsink,
                  tp=512, tg=2)
        ctx.ar_release()


def emit_l1a(ctx, src, dst):
    p = ctx.p
    bo = ctx.load_vec_fm("l1a_bo", ctx.inp("L1a_bo"), D)
    wqt = wsrc_2d(ctx, "L1a_WqT")
    wkt = wsrc_2d(ctx, "L1a_WkT")
    wvt = wsrc_2d(ctx, "L1a_WvT")
    wot = wsrc_2d(ctx, "L1a_WoT")
    cosd = ctx.inp("l1a_cos")
    sind = ctx.inp("l1a_sin")
    rot = ctx.const_sb("c_rotl1", ctx.host["c_rotl1_arr"], BF16)
    for s in range(2):
        st = s * SEQ
        ctx.ar_mark()
        vtm = Buf(ctx, SEQ // 128, D, BF16)
        qfm = Buf(ctx, NCH, SEQ, BF16)
        kfm = Buf(ctx, NCH, SEQ, BF16)
        ctx.ar_mark()
        xn = Buf(ctx, NCH, SEQ, BF16)
        emit_ln(ctx, src, xn, st, st + SEQ, dst_t0=st)
        xf = xfn_buf(xn)
        make_vtm(ctx, "l1a_v", xf, wvt, vtm)
        ctx.ar_mark()
        cosb = Buf(ctx, 1, 512, F32)
        sinb = Buf(ctx, 1, 512, F32)
        t1 = Buf(ctx, 1, 512, F32)
        t2 = Buf(ctx, 1, 512, F32)
        for c in range(NCH):
            for t4 in range(0, SEQ, 512):
                p.dma("act", cosb.ap(0), cosd[c * 128:(c + 1) * 128, t4:t4 + 512],
                      writes=cosb.keys(0), tag="l1a_cl")
                p.dma("act", sinb.ap(0), sind[c * 128:(c + 1) * 128, t4:t4 + 512],
                      writes=sinb.keys(0), tag="l1a_sl")
                rb, rbk = ctx.psb[6 + (t4 // 512) % 2], ctx.pskey[6 + (t4 // 512) % 2]
                p.op("pe", lambda te, _b=rb, _c=c, _t=t4: te.matmul(
                    _b[:], rot[:], xn.ap(_c, _t, 512), start=True, stop=True),
                    reads=xn.keys(c, t4, 512) + [("c_rotl1_sb", None)],
                    writes=[(rbk, None)], tag="l1a_rot")
                p.op("dve", lambda ve, _b=rb: ve.tensor_mul(
                    t1.ap(0, 0, 512), _b[:], sinb.ap(0, 0, 512)),
                    reads=[(rbk, None)] + sinb.keys(0),
                    writes=t1.keys(0), tag="l1a_r1")
                p.op("dve", lambda ve, _c=c, _t=t4: ve.tensor_mul(
                    t2.ap(0, 0, 512), xn.ap(_c, _t, 512), cosb.ap(0, 0, 512)),
                    reads=xn.keys(c, t4, 512) + cosb.keys(0),
                    writes=t2.keys(0), tag="l1a_r2")
                p.op("dve", lambda ve, _c=c, _t=t4: ve.tensor_add(
                    xn.ap(_c, _t, 512), t1.ap(0, 0, 512), t2.ap(0, 0, 512)),
                    reads=t1.keys(0) + t2.keys(0),
                    writes=xn.keys(c, t4, 512), tag="l1a_r3")
        ctx.ar_release()
        linear_fm(ctx, "l1a_q", xf, D, D, SEQ, wqt,
                  sink_to_buf(ctx, "l1a_q", qfm), tp=512, tg=2)
        linear_fm(ctx, "l1a_k", xf, D, D, SEQ, wkt,
                  sink_to_buf(ctx, "l1a_k", kfm), tp=512, tg=2)
        ctx.ar_release()
        cxf = Buf(ctx, NCH, SEQ, BF16)

        def hacc(buf):
            def f(h, t0, tl):
                c, off = h // 2, (h % 2) * 64
                return buf.ap(c, t0, tl)[off:off + 64], buf.keys(c, t0, tl)
            return f

        attn_softmax(ctx, "l1a", [(hacc(qfm), hacc(kfm), 64)], vtm, cxf,
                     causal=True, scale=0.125)
        rsink = sink_residual(ctx, "l1a_o", src, dst, bias_sb=bo,
                              bias_key="l1a_bo", toff=st)
        linear_fm(ctx, "l1a_o", xfn_buf(cxf), D, D, SEQ, wot, rsink,
                  tp=512, tg=2)
        ctx.ar_release()



L3A_DBG = set()


def emit_l3a(ctx, src, dst):
    p = ctx.p
    KVR, QR = 256, 512
    wkv = wsrc_2d(ctx, "L3a_Wkv")
    wkdec = wsrc_2d(ctx, "L3a_Wkdec")
    wvdec = wsrc_2d(ctx, "L3a_Wvdec")
    wq = wsrc_2d(ctx, "L3a_Wq")
    wqdec = wsrc_2d(ctx, "L3a_Wqdec")
    wkrope = wsrc_2d(ctx, "L3a_WkropeP")
    wqrope = wsrc_2d(ctx, "L3a_WqropeP")
    wo = wsrc_2d(ctx, "L3a_Wo")
    gkv = ctx.load_vec_fm("l3a_gkv", ctx.inp("L3a_gkv"), KVR)
    rot32 = ctx.const_sb("c_rot32", ctx.host["c_rot32_arr"], BF16)
    cosd = ctx.inp("l3a_cos")
    sind = ctx.inp("l3a_sin")
    cxd = ctx.dram_scratch("l3a_cx", [D, SEQ], BF16)
    SCALE = 1.0 / float(np.sqrt(96.0))

    def rope_rotate(buf, nch_, tag):
        ctx.ar_mark()
        t1 = Buf(ctx, 1, 512, F32)
        t2 = Buf(ctx, 1, 512, F32)
        for c in range(nch_):
            for off in (0, 64):
                for t4 in range(0, SEQ, 512):
                    rb, rbk = ctx.psb[6], ctx.pskey[6]
                    p.op("pe", lambda te, _b=rb, _c=c, _o=off, _t=t4: te.matmul(
                        _b[0:32, 0:512], rot32[_o:_o + 32, 0:32],
                        buf.ap(_c, _t, 512)[_o:_o + 32], start=True, stop=True),
                        reads=buf.keys(c, t4, 512) + [("c_rot32_sb", None)],
                        writes=[(rbk, None)], tag=tag + "rt")
                    p.op("dve", lambda ve, _b=rb, _o=off, _t=t4: ve.tensor_mul(
                        t1.ap(0, 0, 512)[0:32], _b[0:32, 0:512],
                        ctx.l3sin.ap(0, _t, 512)[_o:_o + 32]),
                        reads=[(rbk, None)] + ctx.l3sin.keys(0, t4, 512),
                        writes=t1.keys(0), tag=tag + "r1")
                    p.op("dve", lambda ve, _c=c, _o=off, _t=t4: ve.tensor_mul(
                        t2.ap(0, 0, 512)[0:32],
                        buf.ap(_c, _t, 512)[_o:_o + 32],
                        ctx.l3cos.ap(0, _t, 512)[_o:_o + 32]),
                        reads=buf.keys(c, t4, 512) + ctx.l3cos.keys(0, t4, 512),
                        writes=t2.keys(0), tag=tag + "r2")
                    p.op("dve", lambda ve, _c=c, _o=off, _t=t4: ve.tensor_add(
                        buf.ap(_c, _t, 512)[_o:_o + 32],
                        t1.ap(0, 0, 512)[0:32], t2.ap(0, 0, 512)[0:32]),
                        reads=t1.keys(0) + t2.keys(0),
                        writes=buf.keys(c, t4, 512), tag=tag + "r3")
        ctx.ar_release()

    for s in range(2):
        st = s * SEQ
        ctx.ar_mark()
        ctx.l3cos = Buf(ctx, 1, SEQ, BF16)
        ctx.l3sin = Buf(ctx, 1, SEQ, BF16)
        if "norope" not in L3A_DBG:
            for off in (0, 64):
                p.dma("pool", ctx.l3cos.ap(0)[off:off + 32], cosd,
                      writes=ctx.l3cos.keys(0), tag="l3a_c")
                p.dma("pool", ctx.l3sin.ap(0)[off:off + 32], sind,
                      writes=ctx.l3sin.keys(0), tag="l3a_s")
        kvn = Buf(ctx, 2, SEQ, BF16)
        qcf = Buf(ctx, 4, SEQ, BF16)
        krp = Buf(ctx, 8, SEQ, BF16)
        ctx.ar_mark()
        xn = Buf(ctx, NCH, SEQ, BF16)
        emit_ln(ctx, src, xn, st, st + SEQ, dst_t0=st)
        xf = xfn_buf(xn)
        kvc32 = Buf(ctx, 2, SEQ, F32)
        linear_fm(ctx, "l3a_kv", xf, D, KVR, SEQ, wkv,
                  sink_to_buf(ctx, "l3a_kv", kvc32), tp=512, tg=2)
        sq = Buf(ctx, 1, 512, F32)
        eap, ek = ctx.fconst(1e-6)
        for t4 in range(0, SEQ, 512):
            bq, bqk = ctx.psb[6], ctx.pskey[6]
            for c in range(2):
                p.op("act", lambda sc, _c=c, _t=t4: sc.activation(
                    sq.ap(0, 0, 512), kvc32.ap(_c, _t, 512), AF.Square),
                    reads=kvc32.keys(c, t4, 512), writes=sq.keys(0),
                    tag="l3a_sq")
                p.op("pe", lambda te, _b=bq, _c=c: te.matmul(
                    _b[0:1, 0:512], ctx.ones_f[:, 0:1], sq.ap(0, 0, 512),
                    start=(_c == 0), stop=(_c == 1)),
                    reads=sq.keys(0) + [("c_ones_sb", None)],
                    writes=[(bqk, None)], tag="l3a_sm")
            p.op("dve", lambda ve, _b=bq: ve.tensor_scalar_mul(
                ctx.rows[:, 0:512], _b[0:1, 0:512], 1.0 / KVR),
                reads=[(bqk, None)], writes=[("rows", 0)], tag="l3a_mn")
            p.op("act", lambda sc, _e=eap: sc.activation(
                ctx.rows[:, 0:512], ctx.rows[:, 0:512], AF.Ln, bias=_e[0:1, 0:1]),
                reads=[("rows", 0), ek], writes=[("rows", 0)], tag="l3a_ln")
            p.op("act", lambda sc: sc.activation(
                ctx.rows[:, 0:512], ctx.rows[:, 0:512], AF.Exp, scale=-0.5),
                reads=[("rows", 0)], writes=[("rows", 0)], tag="l3a_ex")
            bb, bbk = ctx.psb[7], ctx.pskey[7]
            p.op("pe", lambda te, _b=bb: te.matmul(
                _b[:, 0:512], ctx.ones_f[0:1, :], ctx.rows[:, 0:512],
                start=True, stop=True),
                reads=[("rows", 0), ("c_ones_sb", None)],
                writes=[(bbk, None)], tag="l3a_bc")
            for c in range(2):
                p.op("dve", lambda ve, _c=c, _t=t4, _b=bb: ve.tensor_mul(
                    sq.ap(0, 0, 512), kvc32.ap(_c, _t, 512), _b[:, 0:512]),
                    reads=kvc32.keys(c, t4, 512) + [(bbk, None)],
                    writes=sq.keys(0), tag="l3a_nr")
                p.op("dve", lambda ve, _c=c, _t=t4: ve.tensor_scalar_mul(
                    kvn.ap(_c, _t, 512), sq.ap(0, 0, 512), gkv[:, _c:_c + 1]),
                    reads=sq.keys(0), sreads=[("l3a_gkv", None)],
                    writes=kvn.keys(c, t4, 512), tag="l3a_g")
        linear_fm(ctx, "l3a_qc", xf, D, QR, SEQ, wq,
                  sink_to_buf(ctx, "l3a_qc", qcf), tp=512, tg=2)
        if "norope" not in L3A_DBG:
            linear_fm(ctx, "l3a_kr", xf, D, 1024, SEQ, wkrope,
                      sink_to_buf(ctx, "l3a_kr", krp), tp=512, tg=2)
        ctx.ar_release()
        if "norope" not in L3A_DBG:
            rope_rotate(krp, 8, "l3a_kr")

        kvf = xfn_buf(kvn)
        qcff = xfn_buf(qcf)
        for grp in range(2):
            ctx.ar_mark()
            kc = Buf(ctx, 4, SEQ, BF16)
            qc2 = Buf(ctx, 4, SEQ, BF16)
            vtm = Buf(ctx, SEQ // 128, 512, BF16)
            qrp = Buf(ctx, 4, SEQ, BF16)
            cxg = Buf(ctx, 4, SEQ, BF16)
            co = grp * 512
            linear_fm(ctx, "l3a_kc", kvf, KVR, 512, SEQ,
                      lambda k0, kl, c0, cl, _co=co: wkdec(k0, kl, _co + c0, cl),
                      sink_to_buf(ctx, "l3a_kc", kc), tp=512, tg=2)
            linear_fm(ctx, "l3a_q2", qcff, QR, 512, SEQ,
                      lambda k0, kl, c0, cl, _co=co: wqdec(k0, kl, _co + c0, cl),
                      sink_to_buf(ctx, "l3a_q2", qc2), tp=512, tg=2)
            make_vtm(ctx, "l3a_v", kvf, lambda k0, kl, c0, cl, _co=co: wvdec(
                k0, kl, _co + c0, cl), vtm, IN=KVR, HALFW=256)
            if "norope" not in L3A_DBG:
                linear_fm(ctx, "l3a_qr", qcff, QR, 512, SEQ,
                          lambda k0, kl, c0, cl, _co=co: wqrope(k0, kl, _co + c0, cl),
                          sink_to_buf(ctx, "l3a_qr", qrp), tp=512, tg=2)
                rope_rotate(qrp, 4, "l3a_qr")

            def cacc(buf):
                def f(h, t0, tl):
                    return (buf.ap(h // 2, t0, tl)[(h % 2) * 64:(h % 2) * 64 + 64],
                            buf.keys(h // 2, t0, tl))
                return f

            def racc(buf, goff=grp * 4):
                def f(h, t0, tl):
                    c = h // 2
                    off = 64 * (h % 2)
                    return (buf.ap(c, t0, tl)[off:off + 64],
                            buf.keys(c, t0, tl))
                return f

            def kracc(h, t0, tl):
                c = grp * 4 + h // 2
                off = 64 * (h % 2)
                return (krp.ap(c, t0, tl)[off:off + 64], krp.keys(c, t0, tl))

            contribs = [(cacc(qc2), cacc(kc), 64)]
            if not ({"norope", "nocontrib"} & L3A_DBG):
                contribs.append((racc(qrp), kracc, 64))
            attn_softmax(ctx, "l3a", contribs, vtm, cxg, nheads=8,
                         causal=True, scale=SCALE)
            for c in range(4):
                p.dma("sp", cxd[grp * 512 + c * 128: grp * 512 + (c + 1) * 128, :],
                      cxg.ap(c), reads=cxg.keys(c),
                      writes=[dk(cxd, grp * 4 + c)], tag="l3a_cxs")
            ctx.ar_release()
        ctx.ar_mark()
        cxall = Buf(ctx, NCH, SEQ, BF16)
        for c in range(NCH):
            p.dma("sp", cxall.ap(c), cxd[c * 128:(c + 1) * 128, :],
                  reads=[dk(cxd, c)], writes=cxall.keys(c), tag="l3a_cxl")
        rsink = sink_residual(ctx, "l3a_o", src, dst, toff=st)
        linear_fm(ctx, "l3a_o", xfn_buf(cxall), D, D, SEQ, wo, rsink,
                  tp=512, tg=2)
        ctx.ar_release()
        ctx.ar_release()



def emit_ln_sb(ctx, srcb, tl, dst, dtoff, eps=1e-5, gcol=None, bcol=None,
               affkeys=()):
    p = ctx.p
    ctx.ar_mark()
    sq = Buf(ctx, 1, tl, F32)
    rows = ctx.rows
    RK = "rows"
    bsum, bsq = ctx.psb[6], ctx.psb[7]
    for c in range(NCH):
        p.op("pe", lambda te, _c=c: te.matmul(
            bsum[0:1, 0:tl], ctx.ones_f[:, 0:1], srcb.ap(_c, 0, tl),
            start=(_c == 0), stop=(_c == NCH - 1)),
            reads=srcb.keys(c) + [("c_ones_sb", None)],
            writes=[(ctx.pskey[6], None)], tag="ln2sum")
        p.op("act", lambda sc, _c=c: sc.activation(
            sq.ap(0, 0, tl), srcb.ap(_c, 0, tl), AF.Square),
            reads=srcb.keys(c), writes=sq.keys(0), tag="ln2sq")
        p.op("pe", lambda te, _c=c: te.matmul(
            bsq[0:1, 0:tl], ctx.ones_f[:, 0:1], sq.ap(0, 0, tl),
            start=(_c == 0), stop=(_c == NCH - 1)),
            reads=sq.keys(0) + [("c_ones_sb", None)],
            writes=[(ctx.pskey[7], None)], tag="ln2ssq")
    p.op("dve", lambda ve: ve.tensor_scalar_mul(
        rows[:, 0:tl], bsum[0:1, 0:tl], 1.0 / D),
        reads=[(ctx.pskey[6], None)], writes=[(RK, 0)], tag="ln2r0")
    p.op("dve", lambda ve: ve.tensor_scalar_mul(
        rows[:, 512:512 + tl], bsq[0:1, 0:tl], 1.0 / D),
        reads=[(ctx.pskey[7], None)], writes=[(RK, 1)], tag="ln2r1")
    p.op("dve", lambda ve: ve.scalar_tensor_tensor(
        rows[:, 1024:1024 + tl], rows[:, 0:tl], -1.0, rows[:, 0:tl],
        ALU.mult, ALU.mult),
        reads=[(RK, 0)], writes=[(RK, 2)], tag="ln2m2")
    p.op("dve", lambda ve: ve.tensor_add(
        rows[:, 512:512 + tl], rows[:, 512:512 + tl], rows[:, 1024:1024 + tl]),
        reads=[(RK, 1), (RK, 2)], writes=[(RK, 1)], tag="ln2var")
    eap, ek = ctx.fconst(eps)
    p.op("act", lambda sc, _e=eap: sc.activation(
        rows[:, 512:512 + tl], rows[:, 512:512 + tl], AF.Ln, bias=_e[0:1, 0:1]),
        reads=[(RK, 1), ek], writes=[(RK, 1)], tag="ln2ln")
    p.op("act", lambda sc: sc.activation(
        rows[:, 512:512 + tl], rows[:, 512:512 + tl], AF.Exp, scale=-0.5),
        reads=[(RK, 1)], writes=[(RK, 1)], tag="ln2ex")
    bm, br = ctx.psb[4], ctx.psb[5]
    p.op("pe", lambda te: te.matmul(
        bm[:, 0:tl], ctx.ones_f[0:1, :], rows[:, 0:tl], start=True, stop=True),
        reads=[(RK, 0), ("c_ones_sb", None)], writes=[(ctx.pskey[4], None)],
        tag="ln2bm")
    p.op("pe", lambda te: te.matmul(
        br[:, 0:tl], ctx.ones_f[0:1, :], rows[:, 512:512 + tl],
        start=True, stop=True),
        reads=[(RK, 1), ("c_ones_sb", None)], writes=[(ctx.pskey[5], None)],
        tag="ln2br")
    for c in range(NCH):
        p.op("dve", lambda ve, _c=c: ve.tensor_sub(
            sq.ap(0, 0, tl), srcb.ap(_c, 0, tl), bm[:, 0:tl]),
            reads=srcb.keys(c) + [(ctx.pskey[4], None)],
            writes=sq.keys(0), tag="ln2sub")
        p.op("dve", lambda ve, _c=c: ve.tensor_mul(
            sq.ap(0, 0, tl), sq.ap(0, 0, tl), br[:, 0:tl]),
            reads=sq.keys(0) + [(ctx.pskey[5], None)],
            writes=sq.keys(0), tag="ln2mul")
        if gcol is not None:
            p.op("dve", lambda ve, _c=c: ve.tensor_scalar(
                dst.ap(_c, dtoff, tl), sq.ap(0, 0, tl),
                gcol[:, _c:_c + 1], bcol[:, _c:_c + 1], ALU.mult, ALU.add),
                reads=sq.keys(0), sreads=list(affkeys),
                writes=dst.keys(c, dtoff, tl), tag="ln2aff")
        else:
            p.op("act", lambda sc, _c=c: sc.activation(
                dst.ap(_c, dtoff, tl), sq.ap(0, 0, tl), AF.Copy),
                reads=sq.keys(0), writes=dst.keys(c, dtoff, tl), tag="ln2cp")
    ctx.ar_release()


L1F_DBG = set()


def emit_l1f(ctx, src, dst):
    p = ctx.p
    ln1g = ctx.load_vec_fm("l1f_ln1g", ctx.inp("L1f_ln1g"), D)
    ln1b = ctx.load_vec_fm("l1f_ln1b", ctx.inp("L1f_ln1b"), D)
    ln2g = ctx.load_vec_fm("l1f_ln2g", ctx.inp("L1f_ln2g"), D)
    ln2b = ctx.load_vec_fm("l1f_ln2b", ctx.inp("L1f_ln2b"), D)
    bq = ctx.load_vec_fm("l1f_bq", ctx.inp("L1f_bq"), D)
    bk = ctx.load_vec_fm("l1f_bk", ctx.inp("L1f_bk"), D)
    bo = ctx.load_vec_fm("l1f_bo", ctx.inp("L1f_bo"), D)
    b1flat = ctx.inp("L1f_b1").rearrange("e n -> (e n)")
    b1c = [ctx.load_vec_fm(f"l1f_b1_{e}", b1flat[e * 4096:(e + 1) * 4096], 4096)
           for e in range(8)]
    bvrow = ctx.const_sb("l1f_bvrow", ctx.host["L1f_bvrow"], BF16)
    gbrowf = ctx.const_sb("l1f_gbrowf", ctx.host["L1f_gbrow"], F32)

    wq = wsrc_2d(ctx, "L1f_wq")
    wk = wsrc_2d(ctx, "L1f_wk")
    wv = wsrc_2d(ctx, "L1f_wv")
    wo = wsrc_2d(ctx, "L1f_wo")
    w1d = ctx.inp("L1f_w1")
    w2d = ctx.inp("L1f_w2")
    xld = ctx.dram_scratch("l1f_t", [D, TCORE], F32)
    xloc = ctx.dram_scratch("l1f_xl", [D, TCORE], F32)
    b2rd = ctx.inp("L1f_b2row")

    for s in range(2):
        st = s * SEQ
        ctx.ar_mark()
        vtm = Buf(ctx, SEQ // 128, D, BF16)
        qfm = Buf(ctx, NCH, SEQ, BF16)
        kfm = Buf(ctx, NCH, SEQ, BF16)
        ctx.ar_mark()
        hn = Buf(ctx, NCH, SEQ, BF16)
        for t in range(st, st + SEQ, 512):
            ctx.ar_mark()
            t32 = Buf(ctx, NCH, 512, F32)
            emit_ln(ctx, src, None, t, t + 512, dst32=t32, dst_t0=t)
            for c in range(NCH):
                p.dma("sp", xld[c * 128:(c + 1) * 128, t:t + 512],
                      t32.ap(c), reads=t32.keys(c),
                      writes=[dk(xld, t // 512)], tag="l1f_ts")
            emit_ln_sb(ctx, t32, 512, hn, t - st, gcol=ln1g, bcol=ln1b,
                       affkeys=[("l1f_ln1g", None), ("l1f_ln1b", None)])
            ctx.ar_release()
        hf = xfn_buf(hn)
        make_vtm(ctx, "l1f_v", hf, wv, vtm, bias_row=bvrow[0:1, :],
                 bias_keys=[("l1f_bvrow_sb", None)])
        linear_fm(ctx, "l1f_q", hf, D, D, SEQ, wq,
                  sink_to_buf(ctx, "l1f_q", qfm, bias_sb=bq, bias_key="l1f_bq"),
                  tp=512, tg=2)
        linear_fm(ctx, "l1f_k", hf, D, D, SEQ, wk,
                  sink_to_buf(ctx, "l1f_k", kfm, bias_sb=bk, bias_key="l1f_bk"),
                  tp=512, tg=2)
        ctx.ar_release()
        cxf = Buf(ctx, NCH, SEQ, BF16)

        def hacc(buf):
            def f(h, t0, tl):
                c, off = h // 2, (h % 2) * 64
                return buf.ap(c, t0, tl)[off:off + 64], buf.keys(c, t0, tl)
            return f

        attn_softmax(ctx, "l1f", [(hacc(qfm), hacc(kfm), 64)], vtm, cxf,
                     causal=False, scale=0.125)
        rsink = sink_residual(ctx, "l1f_o", xld, xloc, bias_sb=bo,
                              bias_key="l1f_bo", toff=st)
        linear_fm(ctx, "l1f_o", xfn_buf(cxf), D, D, SEQ, wo, rsink,
                  tp=512, tg=2)
        ctx.ar_release()

    if "stop_xloc" in L1F_DBG:
        for c in range(NCH):
            for t in range(0, TCORE, 512):
                p.dma("sp", dst[c * 128:(c + 1) * 128, t:t + 512],
                      xloc[c * 128:(c + 1) * 128, t:t + 512],
                      reads=[dk(xloc, t // 512)], writes=[dk(dst, t // 512)],
                      tag="dbgcp")
        return
    for s in range(2):
        st = s * SEQ
        ctx.ar_mark()
        b2rb = Buf(ctx, 1, 8192, BF16)
        p.dma("pool", b2rb.ap(0, 0, 8192)[0:1], b2rd,
              writes=b2rb.keys(0), tag="l1f_b2l")
        hn2 = Buf(ctx, NCH, SEQ, BF16)
        masks = Buf(ctx, SEQ // 128, 8, F32, hot=True)
        lg32 = Buf(ctx, SEQ // 128, 8, F32, hot=True)
        gatePf = ctx.const_sb("l1f_gatePf", ctx.host["L1f_gateP"], F32) \
            if s == 0 else ctx._l1fg
        ctx._l1fg = gatePf
        for t in range(st, st + SEQ, 512):
            emit_ln(ctx, xloc, hn2, t, t + 512, dst_t0=st, gcol=ln2g,
                    bcol=ln2b, affkeys=[("l1f_ln2g", None), ("l1f_ln2b", None)],
                    route=(gatePf, [("l1f_gatePf_sb", None)], lg32, 0))
        if "stop_hn2" in L1F_DBG:
            for c in range(NCH):
                p.dma("pool", dst[c * 128:(c + 1) * 128, st:st + SEQ],
                      hn2.ap(c), reads=hn2.keys(c),
                      writes=[dk(dst, 100 + s * 8 + c)], tag="dbg2")
            ctx.ar_release()
            continue
        mcol = Buf(ctx, 1, 8, F32, hot=True)
        lsb = Buf(ctx, 1, 8, F32, hot=True)
        gbbp, gbk_ = ctx.psb[6], ctx.pskey[6]
        gbb1 = Buf(ctx, 1, 8, F32, hot=True)
        p.op("pe", lambda te: te.matmul(
            gbbp[:, 0:8], ctx.ones_f[0:1, :], gbrowf[0:1, 0:8],
            start=True, stop=True),
            reads=[("l1f_gbrowf_sb", None), ("c_ones_sb", None)],
            writes=[(gbk_, None)], tag="l1f_gbb")
        p.op("dve", lambda ve: ve.tensor_copy(gbb1.ap(0, 0, 8), gbbp[:, 0:8]),
             reads=[(gbk_, None)], writes=gbb1.keys(0), tag="l1f_gbc")
        for ts in range(SEQ // 128):
            p.op("dve", lambda ve, _ts=ts: ve.tensor_add(
                lsb.ap(0, 0, 8), lg32.ap(_ts, 0, 8), gbb1.ap(0, 0, 8)),
                reads=lg32.keys(ts, 0, 8) + gbb1.keys(0), writes=lsb.keys(0),
                tag="l1f_gcp")
            p.op("dve", lambda ve: ve.reduce_max(
                mcol.ap(0, 0, 1), lsb.ap(0, 0, 8), axis=AX.X),
                reads=lsb.keys(0), writes=mcol.keys(0), tag="l1f_gmx")
            if "dump_logits" in L1F_DBG:
                p.op("dve", lambda ve, _ts=ts: ve.tensor_copy(
                    masks.ap(_ts), lsb.ap(0, 0, 8)),
                    reads=lsb.keys(0), writes=masks.keys(ts), tag="dbgl")
            else:
                p.op("dve", lambda ve, _ts=ts: ve.tensor_scalar(
                    masks.ap(_ts), lsb.ap(0, 0, 8), mcol.ap(0, 0, 1), 0.0,
                    ALU.is_ge, ALU.add),
                    reads=lsb.keys(0), sreads=mcol.keys(0),
                    writes=masks.keys(ts), tag="l1f_gmk")
        if "stop_masks" in L1F_DBG:
            for ts in range(SEQ // 128):
                p.dma("sp", dst[0:128, st + ts * 8: st + ts * 8 + 8],
                      masks.ap(ts), reads=masks.keys(ts),
                      writes=[dk(dst, 200 + s * 16 + ts)], tag="dbgm")
            ctx.ar_release()
            continue
        hbuf = Buf(ctx, 4096 // 128, 512, BF16)
        ytmp = Buf(ctx, 1, 1024, F32)
        acc = Buf(ctx, 4, 1024, F32)
        xl32 = Buf(ctx, 1, 512, F32)
        xr32 = Buf(ctx, 1, 512, F32)
        for tb in range(0, SEQ, 512):
            for e in range(8):
                linear_fm(ctx, "l1f_h", lambda k0, kl, t0, tl, _tb=tb: (
                    hn2.ap(k0 // 128, _tb + t0, tl),
                    hn2.keys(k0 // 128, _tb + t0, tl)),
                    D, 4096, 512,
                    lambda k0, kl, c0, cl, _e=e: (
                        w1d[_e, k0:k0 + kl, c0:c0 + cl], []),
                    sink_to_buf(ctx, "l1f_h", hbuf, bias_sb=b1c[e],
                                bias_key=f"l1f_b1_{e}", act=AF.Relu),
                    tp=512, tg=1)
                for f in range(32):
                    wta, wka = ctx.wtile()
                    wtb, wkb = ctx.wtile()
                    p.dma("pool", wta[:, 0:512], w2d[e, f * 128:(f + 1) * 128, 0:512],
                          writes=[(wka, None)], tag="l1f_w2a")
                    p.dma("pool", wtb[:, 0:512], w2d[e, f * 128:(f + 1) * 128, 512:1024],
                          writes=[(wkb, None)], tag="l1f_w2b")
                    for ts in range(4):
                        for half, (wt, wkk) in enumerate(((wta, wka), (wtb, wkb))):
                            bank = ctx.psb[ts * 2 + half]
                            bkey = ctx.pskey[ts * 2 + half]
                            p.op("pe", lambda te, _b=bank, _w=wt, _f=f, _ts=ts:
                                 te.matmul(
                                     _b[:], hbuf.ap(_f, _ts * 128, 128),
                                     _w[:, 0:512], start=(_f == 0), stop=False),
                                 reads=[(wkk, None)] + hbuf.keys(f, ts * 128, 128),
                                 writes=[(bkey, None)], tag="l1f_ymm")
                for ts in range(4):
                    for half in range(2):
                        bank = ctx.psb[ts * 2 + half]
                        bkey = ctx.pskey[ts * 2 + half]
                        p.op("pe", lambda te, _b=bank, _h=half, _e=e: te.matmul(
                            _b[:], ctx.ones_b[0:1, :],
                            b2rb.ap(0, _e * 1024 + _h * 512, 512)[0:1],
                            start=False, stop=True),
                            reads=b2rb.keys(0) + [("c_onesb_sb", None)],
                            writes=[(bkey, None)], tag="l1f_yb")
                        gts = (tb // 128) + ts
                        p.op("act", lambda sc, _b=bank, _h=half, _gts=gts, _e=e:
                             sc.activation(
                                 ytmp.ap(0, _h * 512, 512), _b[:], AF.Identity,
                                 scale=masks.ap(_gts, _e, 1)),
                             reads=[(bkey, None)] + masks.keys(gts, e, 1),
                             writes=ytmp.keys(0, half * 512, 512),
                             tag="l1f_yev")
                    if e == 0:
                        p.op("dve", lambda ve, _ts=ts: ve.tensor_copy(
                            acc.ap(_ts), ytmp.ap(0)),
                            reads=ytmp.keys(0), writes=acc.keys(ts),
                            tag="l1f_acc0")
                    else:
                        p.op("dve", lambda ve, _ts=ts: ve.tensor_add(
                            acc.ap(_ts), acc.ap(_ts), ytmp.ap(0)),
                            reads=ytmp.keys(0) + acc.keys(ts),
                            writes=acc.keys(ts), tag="l1f_acc")
            for c in range(NCH):
                if "moe_only" in L1F_DBG:
                    p.op("dve", lambda ve: ve.memset(xl32.ap(0), 0.0),
                         writes=xl32.keys(0), tag="dbg0")
                else:
                    p.dma("act", xl32.ap(0), xloc[c * 128:(c + 1) * 128, st + tb:st + tb + 512],
                          reads=[dk(xloc, (st + tb) // 512)],
                          writes=xl32.keys(0), tag="l1f_rl1")
                p.dma("act", xr32.ap(0), src[c * 128:(c + 1) * 128, st + tb:st + tb + 512],
                      reads=[dk(src, (st + tb) // 512)],
                      writes=xr32.keys(0), tag="l1f_rl2")
                resb = Buf(ctx, 1, 512, F32) if c == -1 else None
                for ts in range(4):
                    tpb, tpbk = ctx.psb[6 + ts % 2], ctx.pskey[6 + ts % 2]
                    p.op("pe", lambda te, _b=tpb, _ts=ts, _c=c: te.transpose(
                        _b[:, 0:128], acc.ap(_ts, _c * 128, 128),
                        ctx.ident_f[:]),
                        reads=acc.keys(ts, c * 128, 128) + [("c_ident_sb", None)],
                        writes=[(tpbk, None)], tag="l1f_tp")
                    p.op("dve", lambda ve, _b=tpb, _ts=ts: ve.tensor_add(
                        xl32.ap(0, _ts * 128, 128), xl32.ap(0, _ts * 128, 128),
                        _b[:, 0:128]),
                        reads=xl32.keys(0) + [(tpbk, None)],
                        writes=xl32.keys(0), tag="l1f_ra1")
                if "moe_only" not in L1F_DBG:
                    p.op("dve", lambda ve: ve.tensor_add(
                        xl32.ap(0), xl32.ap(0), xr32.ap(0)),
                        reads=xl32.keys(0) + xr32.keys(0),
                        writes=xl32.keys(0), tag="l1f_ra2")
                p.dma("sp", dst[c * 128:(c + 1) * 128, st + tb:st + tb + 512],
                      xl32.ap(0), reads=xl32.keys(0),
                      writes=[dk(dst, (st + tb) // 512)], tag="l1f_rs")
        ctx.ar_release()



def moe_y_acc(ctx, tag, hbuf, HC, w2src, b2row_ap, b2row_keys, scale_fn,
              acc, first, ytmp, hlens=None):
    p = ctx.p
    for hc in range(HC):
        klen = 128 if hlens is None else hlens[hc]
        wta, wka = ctx.wtile()
        wtb, wkb = ctx.wtile()
        sa, ska = w2src(hc, klen, 0)
        sb2, skb = w2src(hc, klen, 1)
        p.dma("pool", wta[0:klen, 0:512], sa, reads=ska, writes=[(wka, None)],
              tag=tag + "w2a")
        p.dma("pool", wtb[0:klen, 0:512], sb2, reads=skb, writes=[(wkb, None)],
              tag=tag + "w2b")
        for ts in range(4):
            for half, (wt, wkk) in enumerate(((wta, wka), (wtb, wkb))):
                bank = ctx.psb[ts * 2 + half]
                bkey = ctx.pskey[ts * 2 + half]
                p.op("pe", lambda te, _b=bank, _w=wt, _hc=hc, _ts=ts,
                     _kl=klen: te.matmul(
                         _b[:], hbuf.ap(_hc, _ts * 128, 128)[0:_kl],
                         _w[0:_kl, 0:512], start=(_hc == 0), stop=False),
                     reads=[(wkk, None)] + hbuf.keys(hc, ts * 128, 128),
                     writes=[(bkey, None)], tag=tag + "ymm")
    for ts in range(4):
        for half in range(2):
            bank = ctx.psb[ts * 2 + half]
            bkey = ctx.pskey[ts * 2 + half]
            p.op("pe", lambda te, _b=bank, _h=half: te.matmul(
                _b[:], ctx.ones_b[0:1, :],
                b2row_ap(half), start=False, stop=True),
                reads=list(b2row_keys) + [("c_onesb_sb", None)],
                writes=[(bkey, None)], tag=tag + "yb")
            sc_ = scale_fn(ts) if scale_fn else None
            p.op("act", lambda sc, _b=bank, _h=half: sc.activation(
                ytmp.ap(0, _h * 512, 512), _b[:], AF.Copy),
                reads=[(bkey, None)],
                writes=ytmp.keys(0, half * 512, 512), tag=tag + "yev")
            if sc_ is not None:
                sap, skeys = sc_
                p.op("dve", lambda ve, _h=half, _s=sap: ve.tensor_scalar_mul(
                    ytmp.ap(0, _h * 512, 512), ytmp.ap(0, _h * 512, 512), _s),
                    reads=ytmp.keys(0, half * 512, 512), sreads=skeys,
                    writes=ytmp.keys(0, half * 512, 512), tag=tag + "ysc")
        if first:
            p.op("dve", lambda ve, _ts=ts: ve.tensor_copy(
                acc.ap(_ts), ytmp.ap(0)),
                reads=ytmp.keys(0), writes=acc.keys(ts), tag=tag + "a0")
        else:
            p.op("dve", lambda ve, _ts=ts: ve.tensor_add(
                acc.ap(_ts), acc.ap(_ts), ytmp.ap(0)),
                reads=ytmp.keys(0) + acc.keys(ts),
                writes=acc.keys(ts), tag=tag + "aa")


def moe_residual(ctx, tag, acc, src, dst, st, tb, extra_dram=None):
    p = ctx.p
    ctx.ar_mark()
    xl32 = Buf(ctx, 1, 512, F32)
    xr32 = Buf(ctx, 1, 512, F32)
    tg = (st + tb) // 512
    for c in range(NCH):
        p.dma("act", xr32.ap(0), src[c * 128:(c + 1) * 128, st + tb:st + tb + 512],
              reads=[dk(src, tg)], writes=xr32.keys(0), tag=tag + "rl")
        if extra_dram is not None:
            p.dma("act", xl32.ap(0),
                  extra_dram[c * 128:(c + 1) * 128, st + tb:st + tb + 512],
                  reads=[dk(extra_dram, tg)], writes=xl32.keys(0), tag=tag + "re")
            p.op("dve", lambda ve: ve.tensor_add(
                xr32.ap(0), xr32.ap(0), xl32.ap(0)),
                reads=xr32.keys(0) + xl32.keys(0), writes=xr32.keys(0),
                tag=tag + "rx")
        for ts in range(4):
            tpb, tpbk = ctx.psb[6 + ts % 2], ctx.pskey[6 + ts % 2]
            p.op("pe", lambda te, _b=tpb, _ts=ts, _c=c: te.transpose(
                _b[:, 0:128], acc.ap(_ts, _c * 128, 128), ctx.ident_f[:]),
                reads=acc.keys(ts, c * 128, 128) + [("c_ident_sb", None)],
                writes=[(tpbk, None)], tag=tag + "tp")
            p.op("dve", lambda ve, _b=tpb, _ts=ts: ve.tensor_add(
                xr32.ap(0, _ts * 128, 128), xr32.ap(0, _ts * 128, 128),
                _b[:, 0:128]),
                reads=xr32.keys(0) + [(tpbk, None)],
                writes=xr32.keys(0), tag=tag + "ta")
        p.dma("sp", dst[c * 128:(c + 1) * 128, st + tb:st + tb + 512],
              xr32.ap(0), reads=xr32.keys(0), writes=[dk(dst, tg)],
              tag=tag + "rs")
    ctx.ar_release()


L2F_DBG = set()


def emit_l2f(ctx, src, dst):
    p = ctx.p
    b1P = [ctx.load_vec_fm(f"l2f_b1_{e}",
                           ctx.inp("L2f_b1P").rearrange("e n -> (e n)")[
                               e * 2048:(e + 1) * 2048], 2048) for e in range(8)]
    b3P = [ctx.load_vec_fm(f"l2f_b3_{e}",
                           ctx.inp("L2f_b3P").rearrange("e n -> (e n)")[
                               e * 2048:(e + 1) * 2048], 2048) for e in range(8)]
    sb1P = ctx.load_vec_fm("l2f_sb1", ctx.inp("L2f_sb1P"), 2048)
    sb3P = ctx.load_vec_fm("l2f_sb3", ctx.inp("L2f_sb3P"), 2048)
    gateP = ctx.const_sb("l2f_gateP", ctx.host["L2f_gateP"], BF16)
    gbrow = ctx.const_sb("l2f_gbrow", ctx.host["L2f_gbrow"], F32)
    w1d = ctx.inp("L2f_w1P")
    w3d = ctx.inp("L2f_w3P")
    w2d = ctx.inp("L2f_w2T")
    sw1 = wsrc_2d(ctx, "L2f_sw1P")
    sw3 = wsrc_2d(ctx, "L2f_sw3P")
    sw2d = ctx.inp("L2f_sw2T")
    b2rd = ctx.inp("L2f_b2row")
    sb2rd = ctx.inp("L2f_sb2row")

    for s in range(2):
        st = s * SEQ
        ctx.ar_mark()
        xn = Buf(ctx, NCH, SEQ, BF16)
        lg32 = Buf(ctx, SEQ // 128, 8, F32, hot=True)
        gatePf = ctx.const_sb("l2f_gatePf", ctx.host["L2f_gateP"], F32) \
            if s == 0 else ctx._l2fg
        ctx._l2fg = gatePf
        emit_ln(ctx, src, xn, st, st + SEQ, dst_t0=st,
                route=(gatePf, [("l2f_gatePf_sb", None)], lg32, 0))
        xf = xfn_buf(xn)
        b2rb = Buf(ctx, 1, 8192 + 1024, BF16)
        p.dma("pool", b2rb.ap(0, 0, 8192)[0:1], b2rd, writes=b2rb.keys(0),
              tag="l2f_bl")
        p.dma("pool", b2rb.ap(0, 8192, 1024)[0:1], sb2rd, writes=b2rb.keys(0),
              tag="l2f_bl2")
        if "stop_lg" in L2F_DBG:
            ytmp0 = Buf(ctx, 1, 512, F32)
            for ts in range(16):
                p.op("dve", lambda ve, _ts=ts: ve.tensor_copy(
                    ytmp0.ap(0, 0, 8), lg32.ap(_ts, 0, 8)),
                    reads=lg32.keys(ts, 0, 8), writes=ytmp0.keys(0), tag="dl0")
                p.dma("sp", dst[0:128, st + ts * 8:st + ts * 8 + 8],
                      ytmp0.ap(0, 0, 8), reads=ytmp0.keys(0),
                      writes=[dk(dst, 500 + s * 16 + ts)], tag="dlg")
            ctx.ar_release()
            continue
        gbb = Buf(ctx, 1, 8, F32, hot=True)
        gb_ps, gb_k = ctx.psb[6], ctx.pskey[6]
        p.op("pe", lambda te: te.matmul(
            gb_ps[:, 0:8], ctx.ones_f[0:1, :], gbrow[0:1, 0:8],
            start=True, stop=True),
            reads=[("l2f_gbrow_sb", None), ("c_ones_sb", None)],
            writes=[(gb_k, None)], tag="l2f_gbb")
        p.op("dve", lambda ve: ve.tensor_copy(gbb.ap(0, 0, 8), gb_ps[:, 0:8]),
             reads=[(gb_k, None)], writes=gbb.keys(0), tag="l2f_gbc")
        mkb = Buf(ctx, 16, 8, F32, hot=True)
        mct = Buf(ctx, 1, 8, F32, hot=True)
        lsb = Buf(ctx, 1, 8, F32, hot=True)
        esb = Buf(ctx, 1, 8, F32, hot=True)
        eap1, ek1 = ctx.fconst(1.0)
        for ts in range(SEQ // 128):
            p.op("dve", lambda ve, _ts=ts: ve.tensor_copy(
                lsb.ap(0, 0, 8), lg32.ap(_ts, 0, 8)),
                reads=lg32.keys(ts, 0, 8), writes=lsb.keys(0), tag="l2f_gcp")
            p.op("dve", lambda ve: ve.reduce_max(
                mct.ap(0, 2, 1), lsb.ap(0, 0, 8), axis=AX.X),
                reads=lsb.keys(0), writes=mct.keys(0), tag="l2f_smx")
            p.op("dve", lambda ve: ve.tensor_scalar_mul(
                mct.ap(0, 3, 1), mct.ap(0, 2, 1), -1.0),
                reads=mct.keys(0), writes=mct.keys(0), tag="l2f_neg")
            p.op("act", lambda sc: sc.activation(
                esb.ap(0, 0, 8), lsb.ap(0, 0, 8), AF.Exp,
                bias=mct.ap(0, 3, 1)),
                reads=lsb.keys(0) + mct.keys(0), writes=esb.keys(0),
                tag="l2f_sex")
            p.op("dve", lambda ve: ve.reduce_sum(
                mct.ap(0, 4, 1), esb.ap(0, 0, 8), axis=AX.X),
                reads=esb.keys(0), writes=mct.keys(0), tag="l2f_ssm")
            p.op("act", lambda sc: sc.activation(
                mct.ap(0, 4, 1), mct.ap(0, 4, 1), AF.Ln),
                reads=mct.keys(0), writes=mct.keys(0), tag="l2f_sln")
            p.op("act", lambda sc: sc.activation(
                mct.ap(0, 4, 1), mct.ap(0, 4, 1), AF.Exp, scale=-1.0),
                reads=mct.keys(0), writes=mct.keys(0), tag="l2f_sev")
            p.op("dve", lambda ve: ve.tensor_scalar_mul(
                lsb.ap(0, 0, 8), esb.ap(0, 0, 8), mct.ap(0, 4, 1)),
                reads=esb.keys(0), sreads=mct.keys(0), writes=lsb.keys(0),
                tag="l2f_ssc")
            p.op("dve", lambda ve: ve.tensor_add(
                esb.ap(0, 0, 8), lsb.ap(0, 0, 8), gbb.ap(0, 0, 8)),
                reads=lsb.keys(0) + gbb.keys(0), writes=esb.keys(0),
                tag="l2f_bia")
            p.op("dve", lambda ve: ve.reduce_max(
                mct.ap(0, 0, 1), esb.ap(0, 0, 8), axis=AX.X),
                reads=esb.keys(0), writes=mct.keys(0), tag="l2f_m1")
            p.op("dve", lambda ve, _ts=ts: ve.tensor_scalar(
                mkb.ap(_ts, 0, 8), esb.ap(0, 0, 8),
                mct.ap(0, 0, 1), 0.0, ALU.is_ge, ALU.add),
                reads=esb.keys(0), sreads=mct.keys(0),
                writes=mkb.keys(ts), tag="l2f_k1")
            p.op("dve", lambda ve, _ts=ts: ve.scalar_tensor_tensor(
                esb.ap(0, 0, 8), mkb.ap(_ts, 0, 8), -1e30,
                esb.ap(0, 0, 8), ALU.mult, ALU.add),
                reads=mkb.keys(ts) + esb.keys(0), writes=esb.keys(0),
                tag="l2f_sup")
            p.op("dve", lambda ve: ve.reduce_max(
                mct.ap(0, 1, 1), esb.ap(0, 0, 8), axis=AX.X),
                reads=esb.keys(0), writes=mct.keys(0), tag="l2f_m2")
            p.op("dve", lambda ve: ve.tensor_scalar(
                esb.ap(0, 0, 8), esb.ap(0, 0, 8), mct.ap(0, 1, 1), 0.0,
                ALU.is_ge, ALU.add),
                reads=esb.keys(0), sreads=mct.keys(0), writes=esb.keys(0),
                tag="l2f_k2")
            p.op("dve", lambda ve, _ts=ts: ve.tensor_add(
                esb.ap(0, 0, 8), esb.ap(0, 0, 8), mkb.ap(_ts, 0, 8)),
                reads=esb.keys(0) + mkb.keys(ts), writes=esb.keys(0),
                tag="l2f_bth")
            p.op("dve", lambda ve, _ts=ts: ve.tensor_mul(
                mkb.ap(_ts, 0, 8), esb.ap(0, 0, 8), lsb.ap(0, 0, 8)),
                reads=esb.keys(0) + lsb.keys(0), writes=mkb.keys(ts),
                tag="l2f_cw")
        if "stop_cw" in L2F_DBG:
            ytmp0 = Buf(ctx, 1, 512, F32)
            for ts in range(16):
                p.op("dve", lambda ve, _ts=ts: ve.tensor_copy(
                    ytmp0.ap(0, 0, 8), mkb.ap(_ts, 0, 8)),
                    reads=mkb.keys(ts),
                    writes=ytmp0.keys(0), tag="dc0")
                p.dma("sp", dst[0:128, st + ts * 8:st + ts * 8 + 8],
                      ytmp0.ap(0, 0, 8), reads=ytmp0.keys(0),
                      writes=[dk(dst, 400 + s * 16 + ts)], tag="dbgcw")
            ctx.ar_release()
            continue
        if "stop_xn2" in L2F_DBG:
            ytmp0 = Buf(ctx, 1, 512, F32)
            for pc in (range(8) if s == 0 else ()):
                p.op("act", lambda sc, _pc=pc: sc.activation(
                    ytmp0.ap(0, 0, 512), xn.ap(_pc, 0, 512), AF.Copy),
                    reads=xn.keys(pc, 0, 512), writes=ytmp0.keys(0), tag="dxc")
                p.dma("sp", dst[0:128, st + pc * 512:st + (pc + 1) * 512],
                      ytmp0.ap(0, 0, 512), reads=ytmp0.keys(0),
                      writes=[dk(dst, 600 + s * 8 + pc)], tag="dxn2")
            ctx.ar_release()
            continue
        h1f = Buf(ctx, 16, 512, F32)
        h3b = Buf(ctx, 16, 512, BF16)
        hg = Buf(ctx, 1, 512, F32)
        hl = Buf(ctx, 1, 512, F32)
        ht = Buf(ctx, 1, 512, F32)
        abuf = Buf(ctx, 8, 512, BF16)
        ytmp = Buf(ctx, 1, 1024, F32)
        acc = Buf(ctx, 4, 1024, F32)
        for tb in range(0, SEQ, 512):
            if "noexp" in L2F_DBG:
                for ts in range(4):
                    p.op("dve", lambda ve, _ts=ts: ve.memset(acc.ap(_ts), 0.0),
                         writes=acc.keys(ts), tag="dbg0")
                moe_residual(ctx, "l2f", acc, src, dst, st, tb)
                continue
            for e in range(9):
                shared = (e == 8)
                if shared:
                    w1s = sw1
                    w3s = sw3
                    b1v, b1k = sb1P, "l2f_sb1"
                    b3v, b3k = sb3P, "l2f_sb3"
                else:
                    w1s = lambda k0, kl, c0, cl, _e=e: (
                        w1d[_e, k0:k0 + kl, c0:c0 + cl], [])
                    w3s = lambda k0, kl, c0, cl, _e=e: (
                        w3d[_e, k0:k0 + kl, c0:c0 + cl], [])
                    b1v, b1k = b1P[e], f"l2f_b1_{e}"
                    b3v, b3k = b3P[e], f"l2f_b3_{e}"
                xloc = lambda k0, kl, t0, tl, _tb=tb: (
                    xn.ap(k0 // 128, _tb + t0, tl),
                    xn.keys(k0 // 128, _tb + t0, tl))
                linear_fm(ctx, "l2f_h1", xloc, D, 2048, 512, w1s,
                          sink_to_buf(ctx, "l2f_h1", h1f, bias_sb=b1v,
                                      bias_key=b1k), tp=512, tg=1)
                linear_fm(ctx, "l2f_h3", xloc, D, 2048, 512, w3s,
                          sink_to_buf(ctx, "l2f_h3", h3b, bias_sb=b3v,
                                      bias_key=b3k), tp=512, tg=1)
                if "dump_h1" in L2F_DBG and tb == 0 and e == 0 and s == 0:
                    for pc in range(8):
                        p.dma("sp", dst[0:128, pc * 512:(pc + 1) * 512],
                              h1f.ap(pc), reads=h1f.keys(pc),
                              writes=[dk(dst, 300 + pc)], tag="dbgh1")
                if "dump_xn" in L2F_DBG and tb == 0 and e == 0 and s == 0:
                    for pc in range(8):
                        p.op("act", lambda sc, _pc=pc: sc.activation(
                            ytmp.ap(0, 0, 512), xn.ap(_pc, 0, 512), AF.Copy),
                            reads=xn.keys(pc, 0, 512), writes=ytmp.keys(0),
                            tag="dbgc")
                        p.dma("sp", dst[0:128, pc * 512:(pc + 1) * 512],
                              ytmp.ap(0, 0, 512), reads=ytmp.keys(0),
                              writes=[dk(dst, 300 + pc)], tag="dbgxn")
                for pc in range(8):
                    p.op("dve", lambda ve, _pc=pc: ve.tensor_mul(
                        hg.ap(0), h1f.ap(_pc), h3b.ap(_pc)),
                        reads=h1f.keys(pc) + h3b.keys(pc), writes=hg.keys(0),
                        tag="l2f_hg")
                    p.op("dve", lambda ve, _pc=pc: ve.tensor_mul(
                        hl.ap(0), h1f.ap(8 + _pc), h3b.ap(8 + _pc)),
                        reads=h1f.keys(8 + pc) + h3b.keys(8 + pc),
                        writes=hl.keys(0), tag="l2f_hl")
                    p.op("dve", lambda ve: ve.tensor_scalar_min(
                        hg.ap(0), hg.ap(0), 7.0),
                        reads=hg.keys(0), writes=hg.keys(0), tag="l2f_cg")
                    p.op("act", lambda sc: sc.activation(
                        ht.ap(0), hg.ap(0), AF.Tanh, scale=0.851),
                        reads=hg.keys(0), writes=ht.keys(0), tag="l2f_th")
                    p.op("dve", lambda ve: ve.tensor_scalar(
                        ht.ap(0), ht.ap(0), 0.5, 0.5, ALU.mult, ALU.add),
                        reads=ht.keys(0), writes=ht.keys(0), tag="l2f_sg")
                    p.op("dve", lambda ve: ve.tensor_mul(
                        hg.ap(0), hg.ap(0), ht.ap(0)),
                        reads=hg.keys(0) + ht.keys(0), writes=hg.keys(0),
                        tag="l2f_gs")
                    p.op("dve", lambda ve: ve.tensor_scalar_min(
                        hl.ap(0), hl.ap(0), 7.0),
                        reads=hl.keys(0), writes=hl.keys(0), tag="l2f_c1")
                    p.op("dve", lambda ve: ve.tensor_scalar_max(
                        hl.ap(0), hl.ap(0), -7.0),
                        reads=hl.keys(0), writes=hl.keys(0), tag="l2f_c2")
                    p.op("dve", lambda ve, _pc=pc: ve.scalar_tensor_tensor(
                        abuf.ap(_pc), hl.ap(0), 1.0, hg.ap(0),
                        ALU.add, ALU.mult),
                        reads=hl.keys(0) + hg.keys(0), writes=abuf.keys(pc),
                        tag="l2f_a")
                if shared:
                    w2s = lambda hc, kl, half: (
                        sw2d[hc * 128:hc * 128 + kl,
                             half * 512:(half + 1) * 512], [])
                    b2a = lambda half: b2rb.ap(0, 8192 + half * 512, 512)[0:1]
                    scf = None
                else:
                    w2s = lambda hc, kl, half, _e=e: (
                        w2d[_e, hc * 128:hc * 128 + kl,
                            half * 512:(half + 1) * 512], [])
                    b2a = lambda half, _e=e: b2rb.ap(
                        0, _e * 1024 + half * 512, 512)[0:1]
                    scf = lambda ts, _e=e, _tb=tb: (
                        mkb.ap(_tb // 128 + ts, _e, 1),
                        mkb.keys(_tb // 128 + ts, _e, 1))
                if "constsc" in L2F_DBG:
                    if not hasattr(ctx, "_csc"):
                        ctx._csc = p.sbuf("cscol", [128, 1], F32)
                        p.op("dve", lambda ve: ve.memset(ctx._csc[:], 0.5),
                             writes=[("cscol", None)], tag="csc")
                    scf = lambda ts: (ctx._csc[:, 0:1], [("cscol", None)])
                moe_y_acc(ctx, "l2f", abuf, 8, w2s, b2a, b2rb.keys(0),
                          None if "noscale" in L2F_DBG else scf,
                          acc, e == 0, ytmp)
                if "oneexp" in L2F_DBG:
                    break
            moe_residual(ctx, "l2f", acc, src, dst, st, tb)
        ctx.ar_release()


def emit_l3f(ctx, src, dst):
    p = ctx.p
    HH = 2736
    HC = (HH + 127) // 128
    hlens = [128] * 21 + [48]
    bgc = [ctx.load_vec_fm(f"l3f_bg_{e}",
                           ctx.inp("L3f_bg").rearrange("e n -> (e n)")[
                               e * HH:(e + 1) * HH], HH) for e in range(8)]
    b1c = [ctx.load_vec_fm(f"l3f_b1_{e}",
                           ctx.inp("L3f_b1").rearrange("e n -> (e n)")[
                               e * HH:(e + 1) * HH], HH) for e in range(8)]
    centPf = ctx.const_sb("l3f_centPf", ctx.host["L3f_centP"], F32)
    wgd = ctx.inp("L3f_Wg")
    w1d = ctx.inp("L3f_W1")
    w2d = ctx.inp("L3f_W2")
    b2rd = ctx.inp("L3f_b2row")

    for s in range(2):
        st = s * SEQ
        ctx.ar_mark()
        xn = Buf(ctx, NCH, SEQ, BF16)
        lg32 = Buf(ctx, SEQ // 128, 8, F32, hot=True)
        emit_ln(ctx, src, xn, st, st + SEQ, dst_t0=st,
                route=(centPf, [("l3f_centPf_sb", None)], lg32, 0))
        b2rb = Buf(ctx, 1, 8192, BF16)
        p.dma("pool", b2rb.ap(0, 0, 8192)[0:1], b2rd, writes=b2rb.keys(0),
              tag="l3f_bl")
        s2row = Buf(ctx, 1, SEQ, F32)
        sqb = Buf(ctx, 1, 512, BF16)
        for t4 in range(0, SEQ, 512):
            bq, bqk = ctx.psb[6], ctx.pskey[6]
            for c in range(NCH):
                p.op("act", lambda sc, _c=c, _t=t4: sc.activation(
                    sqb.ap(0), xn.ap(_c, _t, 512), AF.Square),
                    reads=xn.keys(c, t4, 512), writes=sqb.keys(0),
                    tag="l3f_sq")
                p.op("pe", lambda te, _b=bq, _c=c: te.matmul(
                    _b[0:1, 0:512], ctx.ones_b[:, 0:1], sqb.ap(0),
                    start=(_c == 0), stop=(_c == NCH - 1)),
                    reads=sqb.keys(0) + [("c_onesb_sb", None)],
                    writes=[(bqk, None)], tag="l3f_s2")
            p.op("act", lambda sc, _b=bq, _t=t4: sc.activation(
                s2row.ap(0, _t, 512)[0:1], _b[0:1, 0:512], AF.Copy),
                reads=[(bqk, None)], writes=s2row.keys(0, t4, 512),
                tag="l3f_s2e")
        ce2b = Buf(ctx, 1, 8, F32, hot=True)
        sq8 = Buf(ctx, 1, 8, F32, hot=True)
        bq, bqk = ctx.psb[6], ctx.pskey[6]
        for c in range(NCH):
            p.op("act", lambda sc, _c=c: sc.activation(
                sq8.ap(0, 0, 8), centPf[:, _c * 8:(_c + 1) * 8], AF.Square),
                reads=[("l3f_centPf_sb", None)], writes=sq8.keys(0),
                tag="l3f_cs")
            p.op("pe", lambda te, _b=bq, _c=c: te.matmul(
                _b[0:1, 0:8], ctx.ones_f[:, 0:1], sq8.ap(0, 0, 8),
                start=(_c == 0), stop=(_c == NCH - 1)),
                reads=sq8.keys(0) + [("c_ones_sb", None)],
                writes=[(bqk, None)], tag="l3f_c2")
        p.op("act", lambda sc, _b=bq: sc.activation(
            sq8.ap(0, 0, 8)[0:1], _b[0:1, 0:8], AF.Copy),
            reads=[(bqk, None)], writes=sq8.keys(0), tag="l3f_c2e")
        bb, bbk = ctx.psb[7], ctx.pskey[7]
        p.op("pe", lambda te, _b=bb: te.matmul(
            _b[:, 0:8], ctx.ones_f[0:1, :], sq8.ap(0, 0, 8)[0:1],
            start=True, stop=True),
            reads=sq8.keys(0) + [("c_ones_sb", None)],
            writes=[(bbk, None)], tag="l3f_c2b")
        p.op("dve", lambda ve, _b=bb: ve.tensor_copy(
            ce2b.ap(0, 0, 8), _b[:, 0:8]),
            reads=[(bbk, None)], writes=ce2b.keys(0), tag="l3f_c2c")
        mkb = Buf(ctx, 16, 8, F32, hot=True)
        mct = Buf(ctx, 1, 8, F32, hot=True)
        dsb = Buf(ctx, 1, 8, F32, hot=True)
        msk1 = Buf(ctx, 1, 8, F32, hot=True)
        eap1, ek1 = ctx.fconst(1.0)
        for ts in range(SEQ // 128):
            tpp, tppk = ctx.psb[6], ctx.pskey[6]
            p.op("pe", lambda te, _b=tpp, _ts=ts: te.transpose(
                _b[:, 0:1], s2row.ap(0, _ts * 128, 128)[0:1],
                ctx.ident_f[0:1, 0:1]),
                reads=s2row.keys(0, ts * 128, 128) + [("c_ident_sb", None)],
                writes=[(tppk, None)], tag="l3f_s2t")
            p.op("dve", lambda ve, _b=tpp: ve.tensor_copy(
                mct.ap(0, 4, 1), _b[:, 0:1]),
                reads=[(tppk, None)], writes=mct.keys(0), tag="l3f_s2c")
            p.op("dve", lambda ve, _ts=ts: ve.tensor_scalar_mul(
                dsb.ap(0, 0, 8), lg32.ap(_ts, 0, 8), -2.0),
                reads=lg32.keys(ts, 0, 8), writes=dsb.keys(0), tag="l3f_d0")
            p.op("dve", lambda ve: ve.tensor_scalar_add(
                dsb.ap(0, 0, 8), dsb.ap(0, 0, 8), mct.ap(0, 4, 1)),
                reads=dsb.keys(0), sreads=mct.keys(0), writes=dsb.keys(0),
                tag="l3f_d1")
            p.op("dve", lambda ve: ve.tensor_add(
                dsb.ap(0, 0, 8), dsb.ap(0, 0, 8), ce2b.ap(0, 0, 8)),
                reads=dsb.keys(0) + ce2b.keys(0), writes=dsb.keys(0),
                tag="l3f_d2")
            p.op("dve", lambda ve: ve.tensor_scalar_max(
                dsb.ap(0, 0, 8), dsb.ap(0, 0, 8), 1e-12),
                reads=dsb.keys(0), writes=dsb.keys(0), tag="l3f_d3")
            p.op("act", lambda sc: sc.activation(
                dsb.ap(0, 0, 8), dsb.ap(0, 0, 8), AF.Ln),
                reads=dsb.keys(0), writes=dsb.keys(0), tag="l3f_dl")
            p.op("act", lambda sc: sc.activation(
                dsb.ap(0, 0, 8), dsb.ap(0, 0, 8), AF.Exp, scale=0.5),
                reads=dsb.keys(0), writes=dsb.keys(0), tag="l3f_de")
            p.op("dve", lambda ve: ve.reduce_max(
                mct.ap(0, 0, 1), dsb.ap(0, 0, 8), axis=AX.X),
                reads=dsb.keys(0), writes=mct.keys(0), tag="l3f_m1")
            p.op("dve", lambda ve: ve.tensor_scalar(
                msk1.ap(0, 0, 8), dsb.ap(0, 0, 8), mct.ap(0, 0, 1), 0.0,
                ALU.is_ge, ALU.add),
                reads=dsb.keys(0), sreads=mct.keys(0), writes=msk1.keys(0),
                tag="l3f_k1")
            p.op("dve", lambda ve: ve.scalar_tensor_tensor(
                dsb.ap(0, 0, 8), msk1.ap(0, 0, 8), -1e30, dsb.ap(0, 0, 8),
                ALU.mult, ALU.add),
                reads=msk1.keys(0) + dsb.keys(0), writes=dsb.keys(0),
                tag="l3f_sup")
            p.op("dve", lambda ve: ve.reduce_max(
                mct.ap(0, 1, 1), dsb.ap(0, 0, 8), axis=AX.X),
                reads=dsb.keys(0), writes=mct.keys(0), tag="l3f_m2")
            p.op("dve", lambda ve: ve.tensor_scalar(
                dsb.ap(0, 0, 8), dsb.ap(0, 0, 8), mct.ap(0, 1, 1), 0.0,
                ALU.is_ge, ALU.add),
                reads=dsb.keys(0), sreads=mct.keys(0), writes=dsb.keys(0),
                tag="l3f_k2")
            p.op("dve", lambda ve: ve.tensor_sub(
                mct.ap(0, 2, 1), mct.ap(0, 1, 1), mct.ap(0, 0, 1)),
                reads=mct.keys(0), writes=mct.keys(0), tag="l3f_dd")
            p.op("act", lambda sc: sc.activation(
                mct.ap(0, 2, 1), mct.ap(0, 2, 1), AF.Exp),
                reads=mct.keys(0), writes=mct.keys(0), tag="l3f_we")
            p.op("act", lambda sc, _e=eap1: sc.activation(
                mct.ap(0, 3, 1), mct.ap(0, 2, 1), AF.Ln, bias=_e[:, 0:1]),
                reads=mct.keys(0) + [ek1], writes=mct.keys(0), tag="l3f_wl")
            p.op("act", lambda sc: sc.activation(
                mct.ap(0, 3, 1), mct.ap(0, 3, 1), AF.Exp, scale=-1.0),
                reads=mct.keys(0), writes=mct.keys(0), tag="l3f_wv")
            p.op("dve", lambda ve: ve.tensor_scalar_mul(
                mct.ap(0, 5, 1), mct.ap(0, 2, 1), mct.ap(0, 3, 1)),
                reads=mct.keys(0), sreads=mct.keys(0), writes=mct.keys(0),
                tag="l3f_w2")
            p.op("dve", lambda ve, _ts=ts: ve.tensor_scalar_mul(
                mkb.ap(_ts, 0, 8), msk1.ap(0, 0, 8),
                mct.ap(0, 3, 1)),
                reads=msk1.keys(0), sreads=mct.keys(0),
                writes=mkb.keys(ts), tag="l3f_cw1")
            p.op("dve", lambda ve, _ts=ts: ve.scalar_tensor_tensor(
                mkb.ap(_ts, 0, 8), dsb.ap(0, 0, 8),
                mct.ap(0, 5, 1), mkb.ap(_ts, 0, 8),
                ALU.mult, ALU.add),
                reads=dsb.keys(0) + mkb.keys(ts), sreads=mct.keys(0),
                writes=mkb.keys(ts), tag="l3f_cw2")
        gf = Buf(ctx, HC, 512, BF16)
        ub = Buf(ctx, HC, 512, BF16)
        ht = Buf(ctx, 1, 512, F32)
        h2 = Buf(ctx, 1, 512, F32)
        hbuf = Buf(ctx, HC, 512, BF16)
        ytmp = Buf(ctx, 1, 1024, F32)
        acc = Buf(ctx, 4, 1024, F32)
        for tb in range(0, SEQ, 512):
            xloc = lambda k0, kl, t0, tl, _tb=tb: (
                xn.ap(k0 // 128, _tb + t0, tl),
                xn.keys(k0 // 128, _tb + t0, tl))
            for e in range(8):
                linear_fm(ctx, "l3f_g", xloc, D, HH, 512,
                          lambda k0, kl, c0, cl, _e=e: (
                              wgd[_e, k0:k0 + kl, c0:c0 + cl], []),
                          sink_to_buf(ctx, "l3f_g", gf, bias_sb=bgc[e],
                                      bias_key=f"l3f_bg_{e}"), tp=512, tg=1)
                linear_fm(ctx, "l3f_u", xloc, D, HH, 512,
                          lambda k0, kl, c0, cl, _e=e: (
                              w1d[_e, k0:k0 + kl, c0:c0 + cl], []),
                          sink_to_buf(ctx, "l3f_u", ub, bias_sb=b1c[e],
                                      bias_key=f"l3f_b1_{e}"), tp=512, tg=1)
                for hc in range(HC):
                    kl = hlens[hc]
                    p.op("act", lambda sc, _hc=hc, _kl=kl: sc.activation(
                        ht.ap(0)[0:_kl], gf.ap(_hc)[0:_kl], AF.Tanh, scale=0.5),
                        reads=gf.keys(hc), writes=ht.keys(0), tag="l3f_t")
                    p.op("dve", lambda ve, _hc=hc, _kl=kl: ve.tensor_mul(
                        h2.ap(0)[0:_kl], gf.ap(_hc)[0:_kl], ub.ap(_hc)[0:_kl]),
                        reads=gf.keys(hc) + ub.keys(hc), writes=h2.keys(0),
                        tag="l3f_p")
                    p.op("dve", lambda ve, _kl=kl: ve.tensor_mul(
                        ht.ap(0)[0:_kl], ht.ap(0)[0:_kl], h2.ap(0)[0:_kl]),
                        reads=ht.keys(0) + h2.keys(0), writes=ht.keys(0),
                        tag="l3f_p2")
                    p.op("dve", lambda ve, _kl=kl: ve.tensor_add(
                        h2.ap(0)[0:_kl], h2.ap(0)[0:_kl], ht.ap(0)[0:_kl]),
                        reads=ht.keys(0) + h2.keys(0), writes=h2.keys(0),
                        tag="l3f_p3")
                    p.op("dve", lambda ve, _hc=hc, _kl=kl: ve.tensor_scalar_mul(
                        hbuf.ap(_hc)[0:_kl], h2.ap(0)[0:_kl], 0.5),
                        reads=h2.keys(0), writes=hbuf.keys(hc), tag="l3f_hb")
                w2s = lambda hc, kl, half, _e=e: (
                    w2d[_e, hc * 128:hc * 128 + kl,
                        half * 512:(half + 1) * 512], [])
                b2a = lambda half, _e=e: b2rb.ap(
                    0, _e * 1024 + half * 512, 512)[0:1]
                scf = lambda ts, _e=e, _tb=tb: (
                    mkb.ap(_tb // 128 + ts, _e, 1),
                    mkb.keys(_tb // 128 + ts, _e, 1))
                moe_y_acc(ctx, "l3f", hbuf, HC, w2s, b2a, b2rb.keys(0), scf,
                          acc, e == 0, ytmp, hlens=hlens)
            moe_residual(ctx, "l3f", acc, src, dst, st, tb)
        ctx.ar_release()



def emit_l0f(ctx, src, dst):
    p = ctx.p
    DT = 0.1
    mu = ctx.load_vec_fm("l0f_mu", ctx.inp("L0f_mu"), D)
    iw = ctx.load_vec_fm("l0f_iw", ctx.inp("L0f_iw"), D)
    rb1 = ctx.load_vec_fm("l0f_rb1", ctx.inp("L0f_rb1"), 256)
    hb1 = ctx.load_vec_fm("l0f_hb1", ctx.inp("L0f_hb1"), 256)
    sb1 = ctx.load_vec_fm("l0f_sb1", ctx.inp("L0f_sb1"), 64)
    sb2 = ctx.load_vec_fm("l0f_sb2", ctx.inp("L0f_sb2"), 3072)
    rfb1 = ctx.load_vec_fm("l0f_rfb1", ctx.inp("L0f_rfb1"), 2048)
    rfb2 = ctx.load_vec_fm("l0f_rfb2", ctx.inp("L0f_rfb2"), D)
    eb1f = ctx.inp("L0f_eb1")
    eb1 = [ctx.load_vec_fm(f"l0f_eb1_{e}", eb1f[e * 64:(e + 1) * 64], 64)
           for e in range(8)]
    eb2f = ctx.inp("L0f_eb2")
    eb2 = [ctx.load_vec_fm(f"l0f_eb2_{e}", eb2f[e * 3072:(e + 1) * 3072], 3072)
           for e in range(8)]
    rw1d = ctx.inp("L0f_rw1")
    rw2d = ctx.inp("L0f_rw2")
    hw1d = ctx.inp("L0f_hw1")
    hw2d = ctx.inp("L0f_hw2")
    sw1d = ctx.inp("L0f_sw1")
    sw2d = ctx.inp("L0f_sw2")
    ew1d = ctx.inp("L0f_ew1")
    ew2d = ctx.inp("L0f_ew2")
    rfw1 = wsrc_2d(ctx, "L0f_rfw1")
    rfw2 = wsrc_2d(ctx, "L0f_rfw2")
    rb2d = ctx.inp("L0f_rb2")
    hb2d = ctx.inp("L0f_hb2")
    swd = ctx.inp("L0f_sw")
    xfd = ctx.dram_scratch("l0f_xf", [D, TCORE], F32)
    one_c, one_k = ctx.fconst(1.0)

    ctx.ar_mark()
    rw1f = Buf(ctx, NCH, 256, F32)
    rw2f = Buf(ctx, 2, 8, F32)
    sw1b = Buf(ctx, 16, 64, BF16)
    sw2b = Buf(ctx, 1, 3072, BF16)
    hw1b = Buf(ctx, NCH, 256, BF16)
    hw2b = Buf(ctx, 2, 1, BF16)
    for k in range(NCH):
        p.dma("sp", rw1f.ap(k), rw1d[k * 128:(k + 1) * 128, :],
              writes=rw1f.keys(k), tag="l0f_w")
        p.dma("pool", hw1b.ap(k), hw1d[k * 128:(k + 1) * 128, :],
              writes=hw1b.keys(k), tag="l0f_w")
    for k in range(2):
        p.dma("sp", rw2f.ap(k), rw2d[k * 128:(k + 1) * 128, :],
              writes=rw2f.keys(k), tag="l0f_w")
        p.dma("pool", hw2b.ap(k), hw2d[k * 128:(k + 1) * 128, :],
              writes=hw2b.keys(k), tag="l0f_w")
    for k in range(16):
        p.dma("pool", sw1b.ap(k), sw1d[k * 128:(k + 1) * 128, :],
              writes=sw1b.keys(k), tag="l0f_w")
    p.dma("pool", sw2b.ap(0)[0:64], sw2d, writes=sw2b.keys(0), tag="l0f_w")
    swv = Buf(ctx, 1, 8, F32, hot=True)
    p.dma("sp", swv.ap(0, 2, 1)[0:1], swd, writes=swv.keys(0), tag="l0f_swl")
    bsw, bswk = ctx.psb[7], ctx.pskey[7]
    p.op("pe", lambda te: te.matmul(
        bsw[:, 0:1], ctx.ones_f[0:1, :], swv.ap(0, 2, 1)[0:1],
        start=True, stop=True),
        reads=swv.keys(0) + [("c_ones_sb", None)], writes=[(bswk, None)],
        tag="l0f_swb")
    p.op("act", lambda sc: sc.activation(
        swv.ap(0, 0, 1), bsw[:, 0:1], AF.Tanh, scale=0.5),
        reads=[(bswk, None)], writes=swv.keys(0), tag="l0f_swt")
    p.op("dve", lambda ve: ve.tensor_scalar(
        swv.ap(0, 0, 1), swv.ap(0, 0, 1), 0.5, 0.5, ALU.mult, ALU.add),
        reads=swv.keys(0), writes=swv.keys(0), tag="l0f_sws")
    p.op("dve", lambda ve: ve.tensor_scalar(
        swv.ap(0, 1, 1), swv.ap(0, 0, 1), -1.0, 1.0, ALU.mult, ALU.add),
        reads=swv.keys(0), writes=swv.keys(0), tag="l0f_osw")
    hb2h = Buf(ctx, 1, 8, F32, hot=True)
    p.dma("sp", hb2h.ap(0, 1, 1)[0:1], hb2d, writes=hb2h.keys(0), tag="l0f_hb2")
    p.op("dve", lambda ve: ve.tensor_scalar_mul(
        hb2h.ap(0, 0, 1)[0:1], hb2h.ap(0, 1, 1)[0:1], 0.5),
        reads=hb2h.keys(0), writes=hb2h.keys(0), tag="l0f_hbh")
    rb2b = Buf(ctx, 1, 8, F32, hot=True)
    rb2s = Buf(ctx, 1, 8, F32, hot=True)
    p.dma("sp", rb2s.ap(0, 0, 8)[0:1], rb2d, writes=rb2s.keys(0), tag="l0f_rb2l")
    brb, brbk = ctx.psb[6], ctx.pskey[6]
    p.op("pe", lambda te: te.matmul(
        brb[:, 0:8], ctx.ones_f[0:1, :], rb2s.ap(0, 0, 8)[0:1],
        start=True, stop=True),
        reads=rb2s.keys(0) + [("c_ones_sb", None)], writes=[(brbk, None)],
        tag="l0f_rb2b")
    p.op("dve", lambda ve: ve.tensor_copy(rb2b.ap(0, 0, 8), brb[:, 0:8]),
         reads=[(brbk, None)], writes=rb2b.keys(0), tag="l0f_rb2c")
    ew1b = [Buf(ctx, 16, 64, BF16) for _ in range(2)]
    ew2b = [Buf(ctx, 1, 3072, BF16) for _ in range(2)]

    for tt in range(TCORE // 512):
        t0 = tt * 512
        ctx.ar_mark()
        s32 = Buf(ctx, NCH, 512, F32)
        v32 = Buf(ctx, NCH, 512, F32)
        emit_ln(ctx, src, None, t0, t0 + 512, dst32=s32, dst_t0=t0)
        for c in range(NCH):
            p.dma("sp", xfd[c * 128:(c + 1) * 128, t0:t0 + 512], s32.ap(c),
                  reads=s32.keys(c), writes=[dk(xfd, t0 // 512)], tag="l0f_xfs")
            p.op("dve", lambda ve, _c=c: ve.memset(v32.ap(_c), 0.0),
                 writes=v32.keys(c), tag="l0f_v0")
        wet = Buf(ctx, 4, 8, F32, hot=True)
        ctx.ar_mark()
        rmid = Buf(ctx, 2, 512, F32)
        for dc in range(2):
            bank, bkey = ctx.psb[4 + dc], ctx.pskey[4 + dc]
            for k in range(NCH):
                p.op("pe", lambda te, _b=bank, _k=k, _dc=dc,
                     _f=(k == 0), _l=(k == NCH - 1): te.matmul(
                         _b[:], rw1f.ap(_k)[:, _dc * 128:(_dc + 1) * 128],
                         s32.ap(_k), start=_f, stop=_l),
                     reads=rw1f.keys(k) + s32.keys(k),
                     writes=[(bkey, None)], tag="l0f_rm")
            p.op("act", lambda sc, _b=bank, _dc=dc: sc.activation(
                rmid.ap(_dc), _b[:], AF.Gelu, bias=rb1[:, _dc:_dc + 1]),
                reads=[(bkey, None), ("l0f_rb1", None)], writes=rmid.keys(dc),
                tag="l0f_rme")
        prb = Buf(ctx, 1, 8, F32, hot=True)
        mk1 = Buf(ctx, 1, 8, F32, hot=True)
        mct = Buf(ctx, 1, 8, F32, hot=True)
        for sub in range(4):
            lb, lbk = ctx.psb[6], ctx.pskey[6]
            for k in range(2):
                p.op("pe", lambda te, _b=lb, _k=k, _s=sub,
                     _f=(k == 0), _l=(k == 1): te.matmul(
                         _b[0:128, 0:8], rmid.ap(_k, _s * 128, 128),
                         rw2f.ap(_k), start=_f, stop=_l),
                     reads=rmid.keys(k, sub * 128, 128) + rw2f.keys(k),
                     writes=[(lbk, None)], tag="l0f_lg")
            p.op("act", lambda sc, _b=lb: sc.activation(
                prb.ap(0, 0, 8), _b[0:128, 0:8], AF.Copy),
                reads=[(lbk, None)], writes=prb.keys(0), tag="l0f_lge")
            p.op("dve", lambda ve: ve.tensor_add(
                prb.ap(0, 0, 8), prb.ap(0, 0, 8), rb2b.ap(0, 0, 8)),
                reads=prb.keys(0) + rb2b.keys(0), writes=prb.keys(0),
                tag="l0f_lgb")
            p.op("dve", lambda ve: ve.reduce_max(
                mct.ap(0, 0, 1), prb.ap(0, 0, 8), axis=AX.X),
                reads=prb.keys(0), writes=mct.keys(0), tag="l0f_sm1")
            p.op("dve", lambda ve: ve.tensor_scalar_mul(
                mct.ap(0, 1, 1), mct.ap(0, 0, 1), -1.0),
                reads=mct.keys(0), writes=mct.keys(0), tag="l0f_sm2")
            p.op("act", lambda sc: sc.activation(
                prb.ap(0, 0, 8), prb.ap(0, 0, 8), AF.Exp,
                bias=mct.ap(0, 1, 1)),
                reads=prb.keys(0) + mct.keys(0), writes=prb.keys(0),
                tag="l0f_sm3")
            p.op("dve", lambda ve: ve.reduce_sum(
                mct.ap(0, 2, 1), prb.ap(0, 0, 8), axis=AX.X),
                reads=prb.keys(0), writes=mct.keys(0), tag="l0f_sm4")
            p.op("act", lambda sc: sc.activation(
                mct.ap(0, 2, 1), mct.ap(0, 2, 1), AF.Ln),
                reads=mct.keys(0), writes=mct.keys(0), tag="l0f_sm5")
            p.op("act", lambda sc: sc.activation(
                mct.ap(0, 2, 1), mct.ap(0, 2, 1), AF.Exp, scale=-1.0),
                reads=mct.keys(0), writes=mct.keys(0), tag="l0f_sm6")
            p.op("dve", lambda ve: ve.tensor_scalar_mul(
                prb.ap(0, 0, 8), prb.ap(0, 0, 8), mct.ap(0, 2, 1)),
                reads=prb.keys(0), sreads=mct.keys(0), writes=prb.keys(0),
                tag="l0f_sm7")
            p.op("dve", lambda ve: ve.reduce_max(
                mct.ap(0, 3, 1), prb.ap(0, 0, 8), axis=AX.X),
                reads=prb.keys(0), writes=mct.keys(0), tag="l0f_t1")
            p.op("dve", lambda ve: ve.tensor_scalar(
                mk1.ap(0, 0, 8), prb.ap(0, 0, 8), mct.ap(0, 3, 1), 0.0,
                ALU.is_ge, ALU.add),
                reads=prb.keys(0), sreads=mct.keys(0), writes=mk1.keys(0),
                tag="l0f_t2")
            p.op("dve", lambda ve: ve.scalar_tensor_tensor(
                prb.ap(0, 0, 8), mk1.ap(0, 0, 8), -2.0, prb.ap(0, 0, 8),
                ALU.mult, ALU.add),
                reads=mk1.keys(0) + prb.keys(0), writes=prb.keys(0),
                tag="l0f_t3")
            p.op("dve", lambda ve: ve.reduce_max(
                mct.ap(0, 4, 1), prb.ap(0, 0, 8), axis=AX.X),
                reads=prb.keys(0), writes=mct.keys(0), tag="l0f_t4")
            p.op("dve", lambda ve: ve.tensor_scalar(
                prb.ap(0, 0, 8), prb.ap(0, 0, 8), mct.ap(0, 4, 1), 0.0,
                ALU.is_ge, ALU.add),
                reads=prb.keys(0), sreads=mct.keys(0), writes=prb.keys(0),
                tag="l0f_t5")
            p.op("dve", lambda ve: ve.tensor_add(
                mct.ap(0, 5, 1), mct.ap(0, 3, 1), mct.ap(0, 4, 1)),
                reads=mct.keys(0), writes=mct.keys(0), tag="l0f_t6")
            p.op("act", lambda sc: sc.activation(
                mct.ap(0, 5, 1), mct.ap(0, 5, 1), AF.Ln),
                reads=mct.keys(0), writes=mct.keys(0), tag="l0f_t7")
            p.op("act", lambda sc: sc.activation(
                mct.ap(0, 5, 1), mct.ap(0, 5, 1), AF.Exp, scale=-1.0),
                reads=mct.keys(0), writes=mct.keys(0), tag="l0f_t8")
            p.op("dve", lambda ve: ve.tensor_scalar_mul(
                mct.ap(0, 6, 1), mct.ap(0, 3, 1), mct.ap(0, 5, 1)),
                reads=mct.keys(0), sreads=mct.keys(0), writes=mct.keys(0),
                tag="l0f_t9")
            p.op("dve", lambda ve: ve.tensor_scalar_mul(
                mct.ap(0, 7, 1), mct.ap(0, 4, 1), mct.ap(0, 5, 1)),
                reads=mct.keys(0), sreads=mct.keys(0), writes=mct.keys(0),
                tag="l0f_ta")
            p.op("dve", lambda ve, _s=sub: ve.tensor_scalar_mul(
                wet.ap(_s, 0, 8), mk1.ap(0, 0, 8), mct.ap(0, 6, 1)),
                reads=mk1.keys(0), sreads=mct.keys(0), writes=wet.keys(sub),
                tag="l0f_tb")
            p.op("dve", lambda ve: ve.tensor_scalar_mul(
                prb.ap(0, 0, 8), prb.ap(0, 0, 8), mct.ap(0, 7, 1)),
                reads=prb.keys(0), sreads=mct.keys(0), writes=prb.keys(0),
                tag="l0f_tc")
            p.op("dve", lambda ve, _s=sub: ve.tensor_add(
                wet.ap(_s, 0, 8), wet.ap(_s, 0, 8), prb.ap(0, 0, 8)),
                reads=wet.keys(sub) + prb.keys(0), writes=wet.keys(sub),
                tag="l0f_td")
        ctx.ar_release()
        tA = Buf(ctx, 1, 512, F32)
        tB = Buf(ctx, 1, 512, F32)
        tC = Buf(ctx, 1, 512, F32)
        tD = Buf(ctx, 1, 512, F32)
        ctx.ar_mark()
        ctxb = Buf(ctx, 16, 512, BF16)
        accx = Buf(ctx, NCH, 512, F32)
        accv = Buf(ctx, NCH, 512, F32)
        cB = Buf(ctx, 24, 512, BF16)
        emid = Buf(ctx, 1, 512, BF16)
        hmid = Buf(ctx, 2, 512, BF16)
        webc = Buf(ctx, 1, 512, F32)
        wrow = Buf(ctx, 1, 512, F32)
        thbc = Buf(ctx, 1, 512, F32)
        hrow = wrow

        def inl_core(cc, vsrc, tag):
            p.op("act", lambda sc, _c=cc: sc.activation(
                tA.ap(0), cB.ap(_c), AF.Tanh, scale=0.5),
                reads=cB.keys(cc), writes=tA.keys(0), tag=tag + "t1")
            p.op("dve", lambda ve, _c=cc: ve.tensor_mul(
                tB.ap(0), tA.ap(0), vsrc.ap(_c)),
                reads=tA.keys(0) + vsrc.keys(cc), writes=tB.keys(0),
                tag=tag + "t2")
            p.op("dve", lambda ve, _c=cc: ve.tensor_add(
                tB.ap(0), tB.ap(0), vsrc.ap(_c)),
                reads=tB.keys(0) + vsrc.keys(cc), writes=tB.keys(0),
                tag=tag + "q1")
            p.op("act", lambda sc, _c=cc: sc.activation(
                tA.ap(0), cB.ap(8 + _c), AF.Abs),
                reads=cB.keys(8 + cc), writes=tA.keys(0), tag=tag + "ab")
            p.op("act", lambda sc: sc.activation(
                tA.ap(0), tA.ap(0), AF.Exp, scale=-1.0),
                reads=tA.keys(0), writes=tA.keys(0), tag=tag + "ex")
            p.op("act", lambda sc, _o=one_c: sc.activation(
                tA.ap(0), tA.ap(0), AF.Ln, bias=_o[:, 0:1]),
                reads=tA.keys(0) + [one_k], writes=tA.keys(0), tag=tag + "ln")
            p.op("act", lambda sc, _c=cc: sc.activation(
                tC.ap(0), cB.ap(8 + _c), AF.Relu),
                reads=cB.keys(8 + cc), writes=tC.keys(0), tag=tag + "rl")
            p.op("dve", lambda ve: ve.tensor_add(
                tA.ap(0), tA.ap(0), tC.ap(0)),
                reads=tA.keys(0) + tC.keys(0), writes=tA.keys(0),
                tag=tag + "sp")
            p.op("dve", lambda ve, _c=cc: ve.tensor_scalar(
                tC.ap(0), s32.ap(_c), mu[:, _c:_c + 1], 0.0,
                ALU.subtract, ALU.add),
                reads=s32.keys(cc), sreads=[("l0f_mu", None)],
                writes=tC.keys(0), tag=tag + "smu")
            p.op("dve", lambda ve: ve.tensor_mul(
                tA.ap(0), tA.ap(0), tC.ap(0)),
                reads=tA.keys(0) + tC.keys(0), writes=tA.keys(0),
                tag=tag + "t3")
            p.op("dve", lambda ve: ve.scalar_tensor_tensor(
                tB.ap(0), tB.ap(0), 0.5, tA.ap(0), ALU.mult, ALU.subtract),
                reads=tB.keys(0) + tA.keys(0), writes=tB.keys(0),
                tag=tag + "vn")
            p.op("act", lambda sc, _c=cc: sc.activation(
                tC.ap(0), cB.ap(16 + _c), AF.Tanh, scale=0.5),
                reads=cB.keys(16 + cc), writes=tC.keys(0), tag=tag + "t4")
            p.op("dve", lambda ve: ve.tensor_scalar_add(
                tC.ap(0), tC.ap(0), 1.0),
                reads=tC.keys(0), writes=tC.keys(0), tag=tag + "t5")

        for it in range(2):
            for c in range(NCH):
                p.op("act", lambda sc, _c=c: sc.activation(
                    ctxb.ap(_c), s32.ap(_c), AF.Copy),
                    reads=s32.keys(c), writes=ctxb.keys(c), tag="l0f_cs")
                p.op("act", lambda sc, _c=c: sc.activation(
                    ctxb.ap(8 + _c), v32.ap(_c), AF.Copy),
                    reads=v32.keys(c), writes=ctxb.keys(8 + c), tag="l0f_cv")
                p.op("dve", lambda ve, _c=c: ve.memset(accx.ap(_c), 0.0),
                     writes=accx.keys(c), tag="l0f_a0")
                p.op("dve", lambda ve, _c=c: ve.memset(accv.ap(_c), 0.0),
                     writes=accv.keys(c), tag="l0f_a1")
            for dc in range(2):
                bank, bkey = ctx.psb[4 + dc], ctx.pskey[4 + dc]
                for k in range(NCH):
                    p.op("pe", lambda te, _b=bank, _k=k, _dc=dc,
                         _f=(k == 0), _l=(k == NCH - 1): te.matmul(
                             _b[:], hw1b.ap(_k)[:, _dc * 128:(_dc + 1) * 128],
                             ctxb.ap(_k), start=_f, stop=_l),
                         reads=hw1b.keys(k) + ctxb.keys(k),
                         writes=[(bkey, None)], tag="l0f_hm")
                p.op("act", lambda sc, _b=bank, _dc=dc: sc.activation(
                    hmid.ap(_dc), _b[:], AF.Gelu, bias=hb1[:, _dc:_dc + 1]),
                    reads=[(bkey, None), ("l0f_hb1", None)],
                    writes=hmid.keys(dc), tag="l0f_hme")
            hb_, hbk_ = ctx.psb[6], ctx.pskey[6]
            for k in range(2):
                p.op("pe", lambda te, _b=hb_, _k=k, _f=(k == 0),
                     _l=(k == 1): te.matmul(
                         _b[0:1, 0:512], hw2b.ap(_k), hmid.ap(_k),
                         start=_f, stop=_l),
                     reads=hw2b.keys(k) + hmid.keys(k),
                     writes=[(hbk_, None)], tag="l0f_hl")
            p.op("act", lambda sc, _b=hb_: sc.activation(
                hrow.ap(0, 0, 512)[0:1], _b[0:1, 0:512], AF.Tanh, scale=0.5,
                bias=hb2h.ap(0, 0, 1)[0:1]),
                reads=[(hbk_, None)] + hb2h.keys(0), writes=hrow.keys(0),
                tag="l0f_ht")
            bt, btk = ctx.psb[7], ctx.pskey[7]
            p.op("pe", lambda te, _b=bt: te.matmul(
                _b[:], ctx.ones_f[0:1, :], hrow.ap(0, 0, 512)[0:1],
                start=True, stop=True),
                reads=hrow.keys(0) + [("c_ones_sb", None)],
                writes=[(btk, None)], tag="l0f_hb")
            p.op("dve", lambda ve, _b=bt: ve.tensor_copy(thbc.ap(0), _b[:]),
                 reads=[(btk, None)], writes=thbc.keys(0), tag="l0f_hc")
            for e in range(8):
                w1t = ew1b[e % 2]
                w2t = ew2b[e % 2]
                if it == 0:
                    for k in range(16):
                        p.dma("pool", w1t.ap(k), ew1d[e, k * 128:(k + 1) * 128, :],
                              writes=w1t.keys(k), tag="l0f_ew1")
                    p.dma("pool", w2t.ap(0)[0:64], ew2d[e], writes=w2t.keys(0),
                          tag="l0f_ew2")
                else:
                    for k in range(16):
                        p.dma("pool", w1t.ap(k), ew1d[e, k * 128:(k + 1) * 128, :],
                              writes=w1t.keys(k), tag="l0f_ew1b")
                    p.dma("pool", w2t.ap(0)[0:64], ew2d[e], writes=w2t.keys(0),
                          tag="l0f_ew2b")
                bw, bwk = ctx.psb[6], ctx.pskey[6]
                for sub in range(4):
                    p.op("pe", lambda te, _b=bw, _s=sub, _e=e: te.matmul(
                        _b[0:1, _s * 128:(_s + 1) * 128], wet.ap(_s, _e, 1),
                        ctx.ident_f[:], start=True, stop=True),
                        reads=wet.keys(sub, e, 1) + [("c_ident_sb", None)],
                        writes=[(bwk, None)], tag="l0f_wt")
                p.op("act", lambda sc, _b=bw: sc.activation(
                    wrow.ap(0, 0, 512)[0:1], _b[0:1, 0:512], AF.Copy),
                    reads=[(bwk, None)], writes=wrow.keys(0), tag="l0f_we")
                bw2, bw2k = ctx.psb[5], ctx.pskey[5]
                p.op("pe", lambda te, _b=bw2: te.matmul(
                    _b[:], ctx.ones_f[0:1, :], wrow.ap(0, 0, 512)[0:1],
                    start=True, stop=True),
                    reads=wrow.keys(0) + [("c_ones_sb", None)],
                    writes=[(bw2k, None)], tag="l0f_wb")
                p.op("dve", lambda ve, _b=bw2: ve.tensor_copy(webc.ap(0), _b[:]),
                     reads=[(bw2k, None)], writes=webc.keys(0), tag="l0f_wc")
                bm_, bmk_ = ctx.psb[4], ctx.pskey[4]
                for k in range(16):
                    p.op("pe", lambda te, _b=bm_, _k=k, _f=(k == 0),
                         _l=(k == 15): te.matmul(
                             _b[0:64, 0:512], w1t.ap(_k), ctxb.ap(_k),
                             start=_f, stop=_l),
                         reads=w1t.keys(k) + ctxb.keys(k),
                         writes=[(bmk_, None)], tag="l0f_em")
                p.op("act", lambda sc, _b=bm_, _e=e: sc.activation(
                    emid.ap(0)[0:64], _b[0:64, 0:512], AF.Gelu,
                    bias=eb1[_e][0:64, 0:1]),
                    reads=[(bmk_, None), (f"l0f_eb1_{e}", None)],
                    writes=emid.keys(0), tag="l0f_eme")
                for dc in range(24):
                    bank = ctx.psb[dc % 2]
                    bkey = ctx.pskey[dc % 2]
                    p.op("pe", lambda te, _b=bank, _dc=dc: te.matmul(
                        _b[:], w2t.ap(0, _dc * 128, 128)[0:64],
                        emid.ap(0)[0:64], start=True, stop=True),
                        reads=w2t.keys(0, dc * 128, 128) + emid.keys(0),
                        writes=[(bkey, None)], tag="l0f_ec")
                    p.op("act", lambda sc, _b=bank, _dc=dc, _e=e: sc.activation(
                        cB.ap(_dc), _b[:], AF.Identity,
                        bias=eb2[_e][:, _dc:_dc + 1]),
                        reads=[(bkey, None), (f"l0f_eb2_{e}", None)],
                        writes=cB.keys(dc), tag="l0f_ece")
                for cc in range(NCH):
                    inl_core(cc, v32, "l0f_i")
                    p.op("dve", lambda ve: ve.tensor_mul(
                        tA.ap(0), tB.ap(0), webc.ap(0)),
                        reads=tB.keys(0) + webc.keys(0), writes=tA.keys(0),
                        tag="l0f_vw")
                    p.op("dve", lambda ve, _c=cc: ve.tensor_add(
                        accv.ap(_c), accv.ap(_c), tA.ap(0)),
                        reads=accv.keys(cc) + tA.keys(0), writes=accv.keys(cc),
                        tag="l0f_av")
                    p.op("dve", lambda ve: ve.tensor_mul(
                        tC.ap(0), tC.ap(0), tB.ap(0)),
                        reads=tC.keys(0) + tB.keys(0), writes=tC.keys(0),
                        tag="l0f_t6")
                    p.op("dve", lambda ve: ve.tensor_mul(
                        tC.ap(0), tC.ap(0), webc.ap(0)),
                        reads=tC.keys(0) + webc.keys(0), writes=tC.keys(0),
                        tag="l0f_xw")
                    p.op("dve", lambda ve, _c=cc: ve.tensor_add(
                        accx.ap(_c), accx.ap(_c), tC.ap(0)),
                        reads=accx.keys(cc) + tC.keys(0), writes=accx.keys(cc),
                        tag="l0f_ax")
            bm_, bmk_ = ctx.psb[4], ctx.pskey[4]
            for k in range(16):
                p.op("pe", lambda te, _b=bm_, _k=k, _f=(k == 0),
                     _l=(k == 15): te.matmul(
                         _b[0:64, 0:512], sw1b.ap(_k), ctxb.ap(_k),
                         start=_f, stop=_l),
                     reads=sw1b.keys(k) + ctxb.keys(k),
                     writes=[(bmk_, None)], tag="l0f_sm")
            p.op("act", lambda sc, _b=bm_: sc.activation(
                emid.ap(0)[0:64], _b[0:64, 0:512], AF.Gelu,
                bias=sb1[0:64, 0:1]),
                reads=[(bmk_, None), ("l0f_sb1", None)], writes=emid.keys(0),
                tag="l0f_sme")
            for dc in range(24):
                bank = ctx.psb[dc % 2]
                bkey = ctx.pskey[dc % 2]
                p.op("pe", lambda te, _b=bank, _dc=dc: te.matmul(
                    _b[:], sw2b.ap(0, _dc * 128, 128)[0:64],
                    emid.ap(0)[0:64], start=True, stop=True),
                    reads=sw2b.keys(0, dc * 128, 128) + emid.keys(0),
                    writes=[(bkey, None)], tag="l0f_sc")
                p.op("act", lambda sc, _b=bank, _dc=dc: sc.activation(
                    cB.ap(_dc), _b[:], AF.Identity, bias=sb2[:, _dc:_dc + 1]),
                    reads=[(bkey, None), ("l0f_sb2", None)],
                    writes=cB.keys(dc), tag="l0f_sce")
            for cc in range(NCH):
                inl_core(cc, v32, "l0f_s")
                p.op("dve", lambda ve: ve.tensor_mul(
                    tC.ap(0), tC.ap(0), tB.ap(0)),
                    reads=tC.keys(0) + tB.keys(0), writes=tC.keys(0),
                    tag="l0f_s6")
                p.op("dve", lambda ve, _c=cc: ve.scalar_tensor_tensor(
                    tC.ap(0), tC.ap(0), 0.05, s32.ap(_c), ALU.mult, ALU.add),
                    reads=tC.keys(0) + s32.keys(cc), writes=tC.keys(0),
                    tag="l0f_sx")
                p.op("dve", lambda ve, _c=cc: ve.scalar_tensor_tensor(
                    tD.ap(0), accx.ap(_c), 0.05, s32.ap(_c), ALU.mult, ALU.add),
                    reads=accx.keys(cc) + s32.keys(cc), writes=tD.keys(0),
                    tag="l0f_rx")
                p.op("dve", lambda ve: ve.tensor_scalar_mul(
                    tC.ap(0), tC.ap(0), swv.ap(0, 0, 1)),
                    reads=tC.keys(0), sreads=swv.keys(0), writes=tC.keys(0),
                    tag="l0f_x1")
                p.op("dve", lambda ve: ve.tensor_scalar_mul(
                    tD.ap(0), tD.ap(0), swv.ap(0, 1, 1)),
                    reads=tD.keys(0), sreads=swv.keys(0), writes=tD.keys(0),
                    tag="l0f_x2a")
                p.op("dve", lambda ve: ve.tensor_add(
                    tC.ap(0), tC.ap(0), tD.ap(0)),
                    reads=tC.keys(0) + tD.keys(0), writes=tC.keys(0),
                    tag="l0f_x2b")
                p.op("dve", lambda ve: ve.tensor_scalar_mul(
                    tB.ap(0), tB.ap(0), swv.ap(0, 0, 1)),
                    reads=tB.keys(0), sreads=swv.keys(0), writes=tB.keys(0),
                    tag="l0f_v1")
                p.op("dve", lambda ve, _c=cc: ve.tensor_scalar_mul(
                    tD.ap(0), accv.ap(_c), swv.ap(0, 1, 1)),
                    reads=accv.keys(cc), sreads=swv.keys(0),
                    writes=tD.keys(0), tag="l0f_v2a")
                p.op("dve", lambda ve: ve.tensor_add(
                    tB.ap(0), tB.ap(0), tD.ap(0)),
                    reads=tB.keys(0) + tD.keys(0), writes=tB.keys(0),
                    tag="l0f_v2b")
                p.op("dve", lambda ve, _c=cc: ve.tensor_sub(
                    tC.ap(0), tC.ap(0), s32.ap(_c)),
                    reads=tC.keys(0) + s32.keys(cc), writes=tC.keys(0),
                    tag="l0f_u1")
                p.op("dve", lambda ve: ve.tensor_mul(
                    tD.ap(0), tC.ap(0), thbc.ap(0)),
                    reads=tC.keys(0) + thbc.keys(0), writes=tD.keys(0),
                    tag="l0f_u2")
                p.op("dve", lambda ve: ve.tensor_add(
                    tD.ap(0), tD.ap(0), tC.ap(0)),
                    reads=tD.keys(0) + tC.keys(0), writes=tD.keys(0),
                    tag="l0f_u3")
                p.op("dve", lambda ve, _c=cc: ve.scalar_tensor_tensor(
                    s32.ap(_c), tD.ap(0), 0.5, s32.ap(_c), ALU.mult, ALU.add),
                    reads=tD.keys(0) + s32.keys(cc), writes=s32.keys(cc),
                    tag="l0f_u4")
                p.op("dve", lambda ve, _c=cc: ve.tensor_sub(
                    tB.ap(0), tB.ap(0), v32.ap(_c)),
                    reads=tB.keys(0) + v32.keys(cc), writes=tB.keys(0),
                    tag="l0f_u5")
                p.op("dve", lambda ve: ve.tensor_mul(
                    tD.ap(0), tB.ap(0), thbc.ap(0)),
                    reads=tB.keys(0) + thbc.keys(0), writes=tD.keys(0),
                    tag="l0f_u6")
                p.op("dve", lambda ve: ve.tensor_add(
                    tD.ap(0), tD.ap(0), tB.ap(0)),
                    reads=tD.keys(0) + tB.keys(0), writes=tD.keys(0),
                    tag="l0f_u7")
                p.op("dve", lambda ve, _c=cc: ve.scalar_tensor_tensor(
                    v32.ap(_c), tD.ap(0), 0.5, v32.ap(_c), ALU.mult, ALU.add),
                    reads=tD.keys(0) + v32.keys(cc), writes=v32.keys(cc),
                    tag="l0f_u8")
        ctx.ar_release()
        sfb = Buf(ctx, NCH, 512, BF16)
        for c in range(NCH):
            p.op("act", lambda sc, _c=c: sc.activation(
                sfb.ap(_c), s32.ap(_c), AF.Copy),
                reads=s32.keys(c), writes=sfb.keys(c), tag="l0f_sf")
        rfm = Buf(ctx, 16, 512, BF16)
        linear_fm(ctx, "l0f_rf", lambda k0, kl, tq, tl: (
            sfb.ap(k0 // 128, tq, tl), sfb.keys(k0 // 128, tq, tl)),
            D, 2048, 512, rfw1,
            sink_to_buf(ctx, "l0f_rf", rfm, bias_sb=rfb1,
                        bias_key="l0f_rfb1", act=AF.Gelu), tp=512, tg=1)
        xfl = Buf(ctx, 1, 512, F32)
        xrl = Buf(ctx, 1, 512, F32)

        def fsink(bank, bkey, cc, dl, tq, tl):
            p.op("act", lambda sc, _b=bank, _cc=cc: sc.activation(
                tA.ap(0), _b[:], AF.Identity, bias=rfb2[:, _cc:_cc + 1]),
                reads=[(bkey, None), ("l0f_rfb2", None)], writes=tA.keys(0),
                tag="l0f_fe")
            p.dma("act", xfl.ap(0), xfd[cc * 128:(cc + 1) * 128, t0:t0 + 512],
                  reads=[dk(xfd, t0 // 512)], writes=xfl.keys(0), tag="l0f_fx")
            p.op("dve", lambda ve, _cc=cc: ve.tensor_scalar_mul(
                tA.ap(0), tA.ap(0), iw[:, _cc:_cc + 1]),
                reads=tA.keys(0), sreads=[("l0f_iw", None)],
                writes=tA.keys(0), tag="l0f_fw")
            p.op("dve", lambda ve: ve.tensor_add(
                tA.ap(0), tA.ap(0), xfl.ap(0)),
                reads=tA.keys(0) + xfl.keys(0), writes=tA.keys(0),
                tag="l0f_fw2")
            p.dma("act", xrl.ap(0), src[cc * 128:(cc + 1) * 128, t0:t0 + 512],
                  reads=[dk(src, t0 // 512)], writes=xrl.keys(0), tag="l0f_fr")
            p.op("dve", lambda ve: ve.tensor_add(
                tA.ap(0), tA.ap(0), xrl.ap(0)),
                reads=tA.keys(0) + xrl.keys(0), writes=tA.keys(0),
                tag="l0f_fa")
            p.dma("sp", dst[cc * 128:(cc + 1) * 128, t0:t0 + 512], tA.ap(0),
                  reads=tA.keys(0), writes=[dk(dst, t0 // 512)], tag="l0f_fs")

        linear_fm(ctx, "l0f_fo", lambda k0, kl, tq, tl: (
            rfm.ap(k0 // 128, tq, tl), rfm.keys(k0 // 128, tq, tl)),
            2048, D, 512, rfw2, fsink, tp=512, tg=1)
        ctx.ar_release()
    ctx.ar_release()


def prep_host(inputs):
    h = {}
    t = lambda a: np.ascontiguousarray(np.asarray(a).astype(np.float32).T)
    c = lambda a: np.ascontiguousarray(np.asarray(a).astype(np.float32))
    h["L0a_WqT"] = t(inputs["L0_attn_Wq"])
    h["L0a_WkT"] = t(inputs["L0_attn_Wk"])
    h["L0a_WvT"] = t(inputs["L0_attn_Wv"])
    h["L0a_WoT"] = t(inputs["L0_attn_Wo"])
    h["L0a_bo"] = c(inputs["L0_attn_bo"])
    h["L2a_WKT"] = t(np.asarray(inputs["L2_attn_W_K"]).reshape(1024, 1024))
    h["L2a_WQT"] = t(np.asarray(inputs["L2_attn_W_Q"]).reshape(1024, 1024))
    h["L2a_WVT"] = t(np.asarray(inputs["L2_attn_W_V"]).reshape(1024, 1024))
    h["L2a_WOT"] = t(inputs["L2_attn_W_O"])
    h["L1a_WqT"] = t(inputs["L1_attn_Wq"])
    h["L1a_WkT"] = t(inputs["L1_attn_Wk"])
    h["L1a_WvT"] = t(inputs["L1_attn_Wv"])
    h["L1a_WoT"] = t(inputs["L1_attn_Wo"])
    h["L1a_bo"] = c(inputs["L1_attn_bo"])
    th = 10000.0 ** (-np.arange(0, D, 2, dtype=np.float64) / D)
    th = np.repeat(th, 2)
    ang = np.arange(SEQ, dtype=np.float64)[None, :] * th[:, None]
    h["l1a_cos"] = np.cos(ang).astype(np.float32)
    h["l1a_sin"] = np.sin(ang).astype(np.float32)
    R = np.zeros((128, 128), np.float32)
    for i in range(64):
        R[2 * i + 1, 2 * i] = -1.0
        R[2 * i, 2 * i + 1] = 1.0
    h["c_rotl1_arr"] = R
    h["L3a_Wkv"] = c(inputs["L3_attn_W_kv"])
    h["L3a_gkv"] = c(inputs["L3_attn_g_kv"])
    h["L3a_Wkdec"] = c(inputs["L3_attn_W_kdec"])
    h["L3a_Wvdec"] = c(inputs["L3_attn_W_vdec"])
    h["L3a_Wq"] = c(inputs["L3_attn_W_q"])
    h["L3a_Wqdec"] = c(inputs["L3_attn_W_qdec"])
    h["L3a_Wo"] = c(inputs["L3_attn_W_o"])

    def rope_pack(w):
        w = np.asarray(w).astype(np.float32)
        out = np.zeros((w.shape[0], 1024), np.float32)
        for hh in range(16):
            j, off = hh // 2, 64 * (hh % 2)
            out[:, j * 128 + off: j * 128 + off + 32] = w[:, hh * 32:(hh + 1) * 32]
        return np.ascontiguousarray(out)
    h["L3a_WkropeP"] = rope_pack(inputs["L3_attn_W_krope"])
    h["L3a_WqropeP"] = rope_pack(inputs["L3_attn_W_qrope"])
    R32 = np.zeros((128, 128), np.float32)
    for base in (0, 64):
        for i in range(16):
            R32[base + 16 + i, i] = -1.0
            R32[base + i, 16 + i] = 1.0
    h["c_rot32_arr"] = R32
    inv = 1.0 / 10000.0 ** (np.arange(0, 32, 2, dtype=np.float64) / 32)
    tt = np.arange(SEQ, dtype=np.float64)
    fr = np.outer(tt, inv)
    emb = np.concatenate([fr, fr], -1)
    h["l3a_cos"] = np.ascontiguousarray(np.cos(emb).T.astype(np.float32))
    h["l3a_sin"] = np.ascontiguousarray(np.sin(emb).T.astype(np.float32))
    h["L1f_ln1g"] = c(inputs["L1_ffn_ln1_g"])
    h["L1f_ln1b"] = c(inputs["L1_ffn_ln1_b"])
    h["L1f_ln2g"] = c(inputs["L1_ffn_ln2_g"])
    h["L1f_ln2b"] = c(inputs["L1_ffn_ln2_b"])
    h["L1f_wq"] = c(inputs["L1_ffn_wq"])
    h["L1f_wk"] = c(inputs["L1_ffn_wk"])
    h["L1f_wv"] = c(inputs["L1_ffn_wv"])
    h["L1f_wo"] = c(inputs["L1_ffn_wo"])
    h["L1f_bq"] = c(inputs["L1_ffn_bq"])
    h["L1f_bk"] = c(inputs["L1_ffn_bk"])
    h["L1f_bo"] = c(inputs["L1_ffn_bo"])
    h["L1f_bvrow"] = c(inputs["L1_ffn_bv"]).reshape(1, 1024)
    h["L1f_w1"] = c(inputs["L1_ffn_w1"])
    h["L1f_b1"] = c(inputs["L1_ffn_b1"])
    h["L1f_w2"] = c(inputs["L1_ffn_w2"])
    h["L1f_b2row"] = c(inputs["L1_ffn_b2"]).reshape(1, 8192)
    gw = c(inputs["L1_ffn_gate_w"])
    h["L1f_gateP"] = np.ascontiguousarray(
        np.concatenate([gw[k * 128:(k + 1) * 128, :] for k in range(8)], axis=1))
    h["L1f_gbrow"] = c(inputs["L1_ffn_gate_b"]).reshape(1, 8)
    par = np.concatenate([np.arange(0, 2048, 2), np.arange(1, 2048, 2)])
    w1 = c(inputs["L2_ffn_w1"])
    w3 = c(inputs["L2_ffn_w3"])
    h["L2f_w1P"] = np.ascontiguousarray(
        w1.transpose(0, 2, 1)[:, :, par])
    h["L2f_w3P"] = np.ascontiguousarray(w3.transpose(0, 2, 1)[:, :, par])
    h["L2f_b1P"] = np.ascontiguousarray(c(inputs["L2_ffn_b1"])[:, par])
    h["L2f_b3P"] = np.ascontiguousarray(c(inputs["L2_ffn_b3"])[:, par])
    h["L2f_w2T"] = np.ascontiguousarray(
        c(inputs["L2_ffn_w2"]).transpose(0, 2, 1))
    h["L2f_b2row"] = c(inputs["L2_ffn_b2"]).reshape(1, 8192)
    h["L2f_sw1P"] = np.ascontiguousarray(t(inputs["L2_ffn_sw1"])[:, par])
    h["L2f_sw3P"] = np.ascontiguousarray(t(inputs["L2_ffn_sw3"])[:, par])
    h["L2f_sb1P"] = np.ascontiguousarray(c(inputs["L2_ffn_sb1"])[par])
    h["L2f_sb3P"] = np.ascontiguousarray(c(inputs["L2_ffn_sb3"])[par])
    h["L2f_sw2T"] = t(inputs["L2_ffn_sw2"])
    h["L2f_sb2row"] = c(inputs["L2_ffn_sb2"]).reshape(1, 1024)
    gw2 = t(inputs["L2_ffn_gate_w"])
    h["L2f_gateP"] = np.ascontiguousarray(
        np.concatenate([gw2[k * 128:(k + 1) * 128, :] for k in range(8)], axis=1))
    h["L2f_gbrow"] = c(inputs["L2_ffn_gate_b"]).reshape(1, 8)
    h["L3f_Wg"] = c(inputs["L3_ffn_Wg"])
    h["L3f_W1"] = c(inputs["L3_ffn_W1"])
    h["L3f_W2"] = c(inputs["L3_ffn_W2"])
    h["L3f_bg"] = c(inputs["L3_ffn_bg"])
    h["L3f_b1"] = c(inputs["L3_ffn_b1"])
    h["L3f_b2row"] = c(inputs["L3_ffn_b2"]).reshape(1, 8192)
    h["L0f_mu"] = c(inputs["L0_ffn_mu"])
    h["L0f_iw"] = c(inputs["L0_ffn_integration_weight"])
    h["L0f_rb1"] = c(inputs["L0_ffn_router_b1"])
    h["L0f_hb1"] = c(inputs["L0_ffn_halt_b1"])
    h["L0f_sb1"] = c(inputs["L0_ffn_shared_b1"])
    h["L0f_sb2"] = c(inputs["L0_ffn_shared_b2"])
    h["L0f_rfb1"] = c(inputs["L0_ffn_refine_b1"])
    h["L0f_rfb2"] = c(inputs["L0_ffn_refine_b2"])
    h["L0f_eb1"] = c(inputs["L0_ffn_expert_b1"]).reshape(-1)
    h["L0f_eb2"] = c(inputs["L0_ffn_expert_b2"]).reshape(-1)
    h["L0f_rw1"] = c(inputs["L0_ffn_router_w1"])
    h["L0f_rw2"] = c(inputs["L0_ffn_router_w2"])
    h["L0f_hw1"] = c(inputs["L0_ffn_halt_w1"])
    h["L0f_hw2"] = c(inputs["L0_ffn_halt_w2"])
    h["L0f_sw1"] = c(inputs["L0_ffn_shared_w1"])
    h["L0f_sw2"] = c(inputs["L0_ffn_shared_w2"])
    h["L0f_ew1"] = c(inputs["L0_ffn_expert_w1"])
    h["L0f_ew2"] = c(inputs["L0_ffn_expert_w2"])
    h["L0f_rfw1"] = c(inputs["L0_ffn_refine_w1"])
    h["L0f_rfw2"] = c(inputs["L0_ffn_refine_w2"])
    h["L0f_rb2"] = c(inputs["L0_ffn_router_b2"]).reshape(1, 8)
    h["L0f_hb2"] = c(inputs["L0_ffn_halt_b2"]).reshape(1, 1)
    h["L0f_sw"] = np.asarray(inputs["L0_ffn_shared_weight"],
                             dtype=np.float32).reshape(1, 1)
    cw3 = t(inputs["L3_ffn_centroid"])
    h["L3f_centP"] = np.ascontiguousarray(
        np.concatenate([cw3[k * 128:(k + 1) * 128, :] for k in range(8)], axis=1))
    return h


STAGE_FNS = {"l0a": emit_l0a, "l2a": emit_l2a, "l1a": emit_l1a, "l3a": emit_l3a, "l1f": emit_l1f, "l2f": emit_l2f, "l3f": emit_l3f, "l0f": emit_l0f}
GROUPS = [["l0a", "l0f"], ["l1a", "l1f"], ["l2a", "l2f"], ["l3a", "l3f"]]


def build_group(stages, host):
    nc = bass.Bass("TRN2", target_bir_lowering=False)
    p = Prog(nc)
    xin = nc.dram_tensor("x_fm", [D, TCORE], F32, kind="ExternalInput").ap()
    xout = nc.dram_tensor("xout_fm", [D, TCORE], F32, kind="ExternalOutput").ap()
    bufs = [xin]
    for i in range(len(stages) - 1):
        bufs.append(nc.dram_tensor(f"xmid{i}", [D, TCORE], F32,
                                   kind="Internal").ap())
    bufs.append(xout)
    ctx = Ctx(nc, p, host)
    ctx.inputs_used.add("x_fm")
    for i, sname in enumerate(stages):
        STAGE_FNS[sname](ctx, bufs[i], bufs[i + 1])
    p.emit()
    return nc, ctx


X_OVERRIDE = None


def kernel(**inputs):
    host = prep_host(inputs)
    x = (np.asarray(X_OVERRIDE, dtype=np.float32) if X_OVERRIDE is not None
         else np.asarray(inputs["L0_attn_x"], dtype=np.float32))
    cur = [np.ascontiguousarray(x[2 * c:2 * c + 2].reshape(TCORE, D).T)
           for c in range(NCORES)]
    for stages in GROUPS:
        nc, ctx = build_group(stages, host)
        names = sorted(ctx.inputs_used - {"x_fm"})
        in_maps = [dict({n: host[n] for n in names}, x_fm=cur[c])
                   for c in range(NCORES)]
        res = run_bass_kernel_spmd(nc, in_maps, core_ids=list(range(NCORES)))
        cur = [res.results[c]["xout_fm"] for c in range(NCORES)]
    out = np.empty((16, SEQ, D), np.float32)
    for c in range(NCORES):
        out[2 * c:2 * c + 2] = cur[c].T.reshape(2, SEQ, D)
    return out
